# version 22 (final)
# baseline (speedup 1.0000x reference)
import numpy as np
import contextlib
import concourse.bass as bass
import concourse.mybir as mybir
from concourse.bass_utils import run_bass_kernel_spmd

F32 = mybir.dt.float32
BF16 = mybir.dt.bfloat16
AX = mybir.AxisListType.X
ALU = mybir.AluOpType
ACT = mybir.ActivationFunctionType

NCORES = 8
DM = 2048
H = 32
DH = 64
WIN = 512
TOWN = 1024
HALO = 512
TALL = TOWN + HALO
NCH = DM // 128
NEG = -1.0e30
POS = 1.0e30
EPS = 1e-5
SCALE = DH ** -0.5
SQ2 = float(np.sqrt(2.0))

ENGS = ("T", "V", "S", "G", "Y")
ENG2SEM = {"T": "sT", "V": "sV", "S": "sS", "G": "sG", "Y": "sY"}
DMA_SEMS = {"dY": ["dY0", "dY1", "dY2", "dY3"],
            "dG": ["dG0", "dG1", "dG2", "dG3"]}
SEMS = ["sT", "sV", "sS", "sG", "sY"] + DMA_SEMS["dY"] + DMA_SEMS["dG"]

BF_N = 77568
F32_N = 12864

KT0 = 0
VT0 = 24576
REG = 49152
STAT_BF = 76032
assert REG + 26880 <= STAT_BF and STAT_BF + 1536 <= BF_N

BN0 = 0
G0 = 10240
ST0 = 12160
DOG0 = ST0 + 256
DRV0 = DOG0 + 224
FLG0 = DRV0 + 192
ID32 = 12840
assert ID32 + 8 <= F32_N


class TV:
    def __init__(self, tensor, pitch, off, width, name=""):
        self.t, self.pitch, self.off, self.w, self.name = tensor, pitch, off, width, name

    def __getitem__(self, idx):
        rs, cs = idx
        r0 = rs.start or 0
        r1 = rs.stop if rs.stop is not None else 128
        c0 = cs.start or 0
        c1 = cs.stop if cs.stop is not None else self.w
        assert 0 <= c0 <= c1 <= self.w, (self.name, c0, c1, self.w)
        return bass.AP(self.t, r0 * self.pitch + self.off + c0,
                       [[self.pitch, r1 - r0], [1, c1 - c0]])


class KB:
    def __init__(self, nc):
        self.nc = nc
        self.qs = {e: [] for e in ENGS}
        self.cnt = {s: 0 for s in SEMS}
        self.hw = {e: {s: 0 for s in SEMS} for e in ENGS}
        self.dmai = {"dY": 0, "dG": 0}

    def dma_total(self, q):
        return [(s, self.cnt[s]) for s in DMA_SEMS[q] if self.cnt[s] > 0]

    def op(self, eng, fn, deps=(), tick=False, dma=None):
        deps = list(deps)
        inc = None
        tok = None
        if dma is not None:
            i = self.dmai[dma]
            self.dmai[dma] = i + 1
            sem = DMA_SEMS[dma][i % 4]
            if self.cnt[sem] > 0:
                deps.append((sem, self.cnt[sem]))
            self.cnt[sem] += 16
            inc = (sem, 16)
            tok = (sem, self.cnt[sem])
        elif tick:
            own = ENG2SEM[eng]
            self.cnt[own] += 1
            inc = (own, 1)
            tok = (own, self.cnt[own])
        waits = []
        own = ENG2SEM[eng]
        for d in deps:
            if d is None:
                continue
            s, v = d
            if s == own and eng in ("T", "Y"):
                continue
            if self.hw[eng][s] >= v:
                continue
            self.hw[eng][s] = v
            waits.append((s, v))
        self.qs[eng].append((fn, waits, inc))
        return tok

    def emit(self, block, sems):
        def mk(ops):
            def run(eng):
                for fn, waits, inc in ops:
                    for s, v in waits:
                        eng.wait_ge(sems[s], v)
                    ins = fn(eng)
                    if inc is not None:
                        ins.then_inc(sems[inc[0]], inc[1])
            return run
        block.tensor(mk(self.qs["T"]))
        block.vector(mk(self.qs["V"]))
        block.scalar(mk(self.qs["S"]))
        block.gpsimd(mk(self.qs["G"]))
        block.sync(mk(self.qs["Y"]))


def dsl(t, r0, r1, c0, c1, cols):
    return bass.AP(t, r0 * cols + c0, [[cols, r1 - r0], [1, c1 - c0]])


def build_program(nc, dbg=False, stop_after=None):
    x_in = nc.dram_tensor("x", [TALL, DM], F32, kind="ExternalInput")
    wq_in = nc.dram_tensor("wq", [DM, DM], F32, kind="ExternalInput")
    wk_in = nc.dram_tensor("wk", [DM, DM], F32, kind="ExternalInput")
    wv_in = nc.dram_tensor("wv", [DM, DM], F32, kind="ExternalInput")
    wo_in = nc.dram_tensor("wo", [DM, DM], F32, kind="ExternalInput")
    wt_in = nc.dram_tensor("wt", [DM, DM], F32, kind="ExternalInput")
    at_in = nc.dram_tensor("at", [DM, 128], F32, kind="ExternalInput")
    bt_in = nc.dram_tensor("bt", [128, DM], F32, kind="ExternalInput")
    rwt_in = nc.dram_tensor("rwt", [DM, 8], F32, kind="ExternalInput")
    dog_in = nc.dram_tensor("dog", [128, 224], F32, kind="ExternalInput")
    snk_in = nc.dram_tensor("snk", [128, 32], F32, kind="ExternalInput")
    cid_in = nc.dram_tensor("cid", [1, 4], F32, kind="ExternalInput")
    y_out = nc.dram_tensor("y", [TOWN, DM], F32, kind="ExternalOutput")
    dkind = "ExternalOutput" if dbg else "Internal"
    q_dram = nc.dram_tensor("qscr", [DM, TOWN], BF16, kind=dkind)
    y1_dram = nc.dram_tensor("y1scr", [TOWN, DM], F32, kind=dkind)
    if dbg:
        d_bneg = nc.dram_tensor("d_bneg", [128, 8192], F32, kind="ExternalOutput")
        d_afm = nc.dram_tensor("d_afm", [128, 16384], BF16, kind="ExternalOutput")
        d_wofm = nc.dram_tensor("d_wofm", [128, 16384], BF16, kind="ExternalOutput")
        d_yfm = nc.dram_tensor("d_yfm", [128, 16384], BF16, kind="ExternalOutput")
        d_ffm = nc.dram_tensor("d_ffm", [128, 16384], BF16, kind="ExternalOutput")
        d_stat = nc.dram_tensor("d_stat", [128, 688], F32, kind="ExternalOutput")
        d_scax = nc.dram_tensor("d_scax", [128, 1024], BF16, kind="ExternalOutput")

    abf = nc.alloc_sbuf_tensor("abf", [128, BF_N], BF16)
    af = nc.alloc_sbuf_tensor("af32", [128, F32_N], F32)

    def bv(off, w, name=""):
        return TV(abf, BF_N, off, w, name)

    def fv(off, w, name=""):
        return TV(af, F32_N, off, w, name)

    def bap(off, dims, row0=0, nrows=128):
        return bass.AP(abf, row0 * BF_N + off, [[BF_N, nrows]] + dims)

    def fap(off, dims, row0=0, nrows=128):
        return bass.AP(af, row0 * F32_N + off, [[F32_N, nrows]] + dims)

    kT = [bv(KT0 + i * TALL, TALL, f"kT{i}") for i in range(NCH)]
    vT = [bv(VT0 + j * DM, DM, f"vT{j}") for j in range(12)]
    xfm = [bv(REG + i * 768, 768, f"xfm{i}") for i in range(NCH)]
    xtm = [bv(REG + 12288 + b * DM, DM, f"xtm{b}") for b in range(2)]
    wsb = [bv(REG + 16384 + b * 4096, 4096, f"wsb{b}") for b in range(2)]
    wsb4 = [bv(REG + 16384 + j * 2048, 2048, f"wsb4_{j}") for j in range(4)]
    qst = [bv(REG + 24576 + j * 768, 768, f"qst{j}") for j in range(2)]
    pbuf = [bv(REG + k * 640, 640, f"pb{k}") for k in range(8)]
    ptsb = [bv(REG + 5120 + b * 640, 640, f"ptsb{b}") for b in range(2)]
    QB0 = REG + 6400
    AFM0 = REG + 10496
    afm = bv(AFM0, 16384, "afm")
    wofm = bv(0, 16384, "wofm")
    yfm = [bv(16384 + i * 1024, 1024, f"yfm{i}") for i in range(NCH)]
    YFM0 = 16384
    ffm = bv(32768, 16384, "ffm")
    FFM0 = 32768
    atb = bv(REG, DM, "atb")
    btb = bv(REG + 2048, DM, "btb")
    rwb = bv(REG + 4096, 128, "rwb")
    RW0 = REG + 4096
    scax = [bv(REG + 4224 + b * 512, 512, f"scax{b}") for b in range(2)]
    wsb2 = [bv(REG + 6400 + b * 2048, 2048, f"wsb2{b}") for b in range(2)]
    sqb = bv(REG + 6400, DM, "sqb")
    idb = bv(STAT_BF, 128, "idb")
    e8b = bv(STAT_BF + 128, 128, "e8")
    onesb = bv(STAT_BF + 256, 128, "ones")
    mrowb = bv(STAT_BF + 384, 1152, "mrow")

    bneg = [fv(BN0 + k * 640, 640, f"bneg{k}") for k in range(16)]
    xb = [fv(b * 2048, DM, f"xb{b}") for b in range(2)]
    r1b = fv(4096, DM, "r1b")
    r1c = fv(8192, DM, "r1c")
    y1b = fv(6144, DM, "y1b")
    gdist = fv(G0, 640, "gdist")
    gscr = fv(G0 + 640, 640, "gscr")
    gscr2 = fv(G0 + 1280, 640, "gscr2")
    lgfm = fv(G0, 512, "lgfm")
    lgtm = fv(G0 + 512, 32, "lgtm")
    rst = fv(G0 + 552, 64, "rst")
    wtm = bv(REG + 5760, 8, "wtm")
    wfm = bv(REG + 5248, 512, "wfm")
    wrepf = fv(G0 + 1136, 512, "wrepf")
    nmv = fv(ST0, 32, "nm")
    nm2v = fv(ST0 + 32, 32, "nm2")
    dsum = fv(ST0 + 64, 32, "dsum")
    tmp32 = fv(ST0 + 96, 32, "tmp32")
    denv = fv(ST0 + 128, 32, "den")
    rdv = fv(ST0 + 160, 32, "rd")
    esnkb = fv(ST0 + 192, 32, "esnk")
    snkb = fv(ST0 + 224, 32, "snk")
    lnst = fv(ST0, 128, "lnst")
    dgp = fv(DOG0, 224, "dgp")
    a1c = fv(DRV0, 32, "a1c")
    nb1 = fv(DRV0 + 32, 32, "nb1")
    a2c = fv(DRV0 + 64, 32, "a2c")
    nb2 = fv(DRV0 + 96, 32, "nb2")
    nega1 = fv(DRV0 + 128, 32, "nega1")
    negc = fv(DRV0 + 160, 32, "negc")
    flagv = fv(FLG0, 8, "flag")
    idb32 = fv(ID32, 8, "idb32")

    kb = KB(nc)
    OP = kb.op
    sems = {}
    ctx = contextlib.ExitStack()
    psS = [ctx.enter_context(nc.psum_tensor(f"psS{i}", [128, 1024], F32)) for i in range(2)]
    psPT = [ctx.enter_context(nc.psum_tensor(f"psPT{i}", [128, 1024], BF16)) for i in range(2)]
    psO = [ctx.enter_context(nc.psum_tensor(f"psO{i}", [128, 512], F32)) for i in range(2)]
    for s in SEMS:
        sems[s] = ctx.enter_context(nc.semaphore(s))

    def psap(t, off, dims, row0=0, nrows=128):
        pitch = 1024 if t in psS or t in psPT else 512
        return bass.AP(t, row0 * pitch + off, [[pitch, nrows]] + dims)

    if dbg:
        hfA = F32_N // 2
        hfB = BF_N // 2
        t_zA = OP("V", lambda v: v.memset(bass.AP(af, 0, [[F32_N, 128], [1, hfA]]),
                                          0.0), tick=True)
        t_zA2 = OP("V", lambda v: v.memset(
            bass.AP(af, hfA, [[F32_N, 128], [1, F32_N - hfA]]), 0.0),
            deps=[t_zA], tick=True)
        t_zB = OP("V", lambda v: v.memset(bass.AP(abf, 0, [[BF_N, 128], [1, hfB]]),
                                          0.0), deps=[t_zA2], tick=True)
        t_zB2 = OP("V", lambda v: v.memset(
            bass.AP(abf, hfB, [[BF_N, 128], [1, BF_N - hfB]]), 0.0),
            deps=[t_zB], tick=True)
        _zdep = [t_zB2]
    else:
        _zdep = []
    gregs = {}

    def f_regs(g):
        gregs["pos"] = g.alloc_register("fillpos")
        gregs["zero"] = g.alloc_register("fillzero")
        g.reg_mov(gregs["pos"], POS)
        return g.reg_mov(gregs["zero"], 0.0)
    xpre = [OP("G", lambda g, tt=tt: g.dma_start(
        out=xtm[tt][:, :],
        in_=dsl(x_in, tt * 128, tt * 128 + 128, 0, DM, DM)),
        deps=list(_zdep), dma="dG") for tt in range(2)]
    OP("G", f_regs, deps=_zdep)

    t_id0 = OP("G", lambda g: g.memset(idb[:, :], 1.0), deps=_zdep, tick=True)
    t_id = OP("G", lambda g: g.affine_select(idb[:, :], idb[:, :], [[-1, 128]],
                                             ALU.is_equal, gregs["zero"], base=0,
                                             channel_multiplier=1),
              deps=[t_id0], tick=True)

    t_i320 = OP("G", lambda g: g.memset(idb32[0:8, :], 1.0), deps=_zdep, tick=True)
    t_id32 = OP("G", lambda g: g.affine_select(idb32[0:8, :], idb32[0:8, :], [[-1, 8]],
                                               ALU.is_equal, gregs["zero"], base=0,
                                               channel_multiplier=1),
               deps=[t_i320], tick=True)

    t_e80 = OP("G", lambda g: g.memset(e8b[0:8, :], 1.0), deps=_zdep, tick=True)
    t_e81 = OP("G", lambda g: g.affine_select(e8b[0:8, :], e8b[0:8, :], [[1, 128]],
                                              ALU.is_ge, gregs["zero"], base=0,
                                              channel_multiplier=-16),
               deps=[t_e80], tick=True)
    t_e8 = OP("G", lambda g: g.affine_select(e8b[0:8, :], e8b[0:8, :], [[-1, 128]],
                                             ALU.is_ge, gregs["zero"], base=15,
                                             channel_multiplier=16),
              deps=[t_e81], tick=True)
    t_ones = OP("G", lambda g: g.memset(onesb[0:1, :], 1.0), deps=_zdep, tick=True)
    t_gd = OP("G", lambda g: g.iota(gdist[:, :], [[-1, 640]], base=512,
                                    channel_multiplier=1,
                                    allow_small_or_imprecise_dtypes=True),
              deps=_zdep, tick=True)
    t_z0 = OP("V", lambda v: v.memset(fap(ST0, [[1, 680]]), 0.0),
              deps=_zdep, tick=True)
    t_dog = OP("Y", lambda y: y.dma_start(out=dgp[:, :], in_=dog_in.ap()),
               deps=[t_z0], dma="dY")
    t_snk = OP("Y", lambda y: y.dma_start(out=snkb[:, :], in_=snk_in.ap()), dma="dY")
    t_cid = OP("Y", lambda y: y.dma_start(out=flagv[0:1, 0:4], in_=cid_in.ap()), dma="dY")

    t_mt0 = OP("G", lambda g: g.memset(gscr[0:1, :], NEG), deps=_zdep, tick=True)
    t_mt0b = OP("G", lambda g: g.memset(gscr2[0:1, :], NEG), deps=[t_mt0], tick=True)
    t_mt1 = OP("G", lambda g: g.affine_select(gscr[0:1, :], gscr[0:1, :], [[-1, 640]],
                                              ALU.is_ge, gregs["zero"], base=511,
                                              channel_multiplier=0),
               deps=[t_mt0], tick=True)
    t_flag = OP("V", lambda v: v.tensor_single_scalar(flagv[0:1, 4:5], flagv[0:1, 0:1],
                                                      0.0, ALU.is_equal),
                deps=[t_cid], tick=True)
    t_mt2 = OP("V", lambda v: v.tensor_scalar_mul(gscr[0:1, :], gscr[0:1, :],
                                                  flagv[0:1, 4:5]),
               deps=[t_mt1, t_flag], tick=True)
    t_mra = OP("V", lambda v: (v.drain(), v.tensor_copy(mrowb[0:1, 0:640],
                                                         gscr[0:1, :]))[-1],
               deps=[t_mt2], tick=True)
    t_mrb = OP("G", lambda g: g.memset(mrowb[0:1, 640:1152], 0.0), deps=_zdep, tick=True)
    t_mrow = t_mra

    t_esnk = OP("S", lambda s: s.activation(esnkb[:, :], snkb[:, :], ACT.Exp),
                deps=[t_snk], tick=True)

    def drv(ps, pm, acoef_t, nb_t, dep0):
        t1 = OP("S", lambda s: s.activation(gscr2[:, 0:32], dgp[:, ps:ps + 32],
                                            ACT.Exp), deps=[t_dog, dep0], tick=True)
        t2 = OP("V", lambda v: v.tensor_scalar_add(gscr2[:, 0:32], gscr2[:, 0:32], 1.0),
                deps=[t1], tick=True)
        t3 = OP("S", lambda s: s.activation(gscr2[:, 0:32], gscr2[:, 0:32], ACT.Ln),
                deps=[t2], tick=True)
        t4 = OP("V", lambda v: v.tensor_scalar_mul(gscr2[:, 0:32], gscr2[:, 0:32], SQ2),
                deps=[t3], tick=True)
        t5 = OP("V", lambda v: (v.drain(), v.reciprocal(acoef_t[:, :], gscr2[:, 0:32]),
                                v.drain())[-1], deps=[t4], tick=True)
        return OP("V", lambda v: v.scalar_tensor_tensor(
            out=nb_t[:, :], in0=dgp[:, pm:pm + 32], scalar=-1.0,
            in1=acoef_t[:, :], op0=ALU.mult, op1=ALU.mult), deps=[t5], tick=True)

    td1 = drv(32, 64, a1c, nb1, t_mt0b)
    td2 = drv(128, 160, a2c, nb2, td1)
    t_na1 = OP("V", lambda v: v.tensor_scalar_mul(nega1[:, :], dgp[:, 0:32], -1.0),
               deps=[t_dog], tick=True)
    t_nc = OP("V", lambda v: v.tensor_scalar_mul(negc[:, :], dgp[:, 192:224], -1.0),
              deps=[t_dog], tick=True)

    def emit_bias_head(h, dst, prev_tf, extra_deps=()):
        base_deps = [td1, td2, t_na1, t_nc, t_gd, t_mra] + list(extra_deps)
        ta = OP("S", lambda s, h=h: s.activation(
            gscr[:, :], gdist[:, :], ACT.Square,
            bias=nb1[:, h:h + 1], scale=a1c[:, h:h + 1]),
            deps=base_deps + [prev_tf], tick=True)
        tb = OP("S", lambda s: s.activation(gscr2[:, :], gscr[:, :], ACT.Exp,
                                            scale=-1.0), deps=[ta], tick=True)
        tc = OP("V", lambda v, h=h: v.tensor_scalar(
            gscr[:, :], gscr2[:, :], nega1[:, h:h + 1], negc[:, h:h + 1],
            ALU.mult, ALU.add), deps=[tb], tick=True)
        td_ = OP("S", lambda s, h=h, dst=dst: s.activation(
            dst[:, :], gdist[:, :], ACT.Square,
            bias=nb2[:, h:h + 1], scale=a2c[:, h:h + 1]),
            deps=base_deps + [prev_tf], tick=True)
        te = OP("S", lambda s, dst=dst: s.activation(dst[:, :], dst[:, :],
                                                     ACT.Exp,
                                                     scale=-1.0), deps=[td_], tick=True)
        tf = OP("V", lambda v, dst=dst, h=h: v.scalar_tensor_tensor(
            out=dst[:, :], in0=dst[:, :], scalar=dgp[:, 96 + h:97 + h],
            in1=gscr[:, :], op0=ALU.mult, op1=ALU.add), deps=[te, tc], tick=True)
        tg1 = OP("G", lambda g, dst=dst: g.affine_select(
            dst[:, :], dst[:, :], [[1, 640]], ALU.is_ge, gregs["pos"],
            base=-1, channel_multiplier=-1), deps=[tf], tick=True)
        tg2 = OP("G", lambda g, dst=dst: g.affine_select(
            dst[:, :], dst[:, :], [[-1, 640]], ALU.is_ge, gregs["pos"],
            base=512, channel_multiplier=1), deps=[tg1], tick=True)
        return tg2, tf

    def build_bias_half(half, extra_deps=()):
        toks = []
        prev_tf = None
        for hh in range(16):
            dst = bneg[hh] if half == 0 else bneg2[hh]
            tg2, prev_tf = emit_bias_head(16 * half + hh, dst, prev_tf, extra_deps)
            toks.append(tg2)
        return toks, prev_tf

    kT_done = []
    vT_done = []
    q_done = []

    def phase_a():
        wsb_free = [None, None]
        wsb4_free = [None, None, None, None]
        qst_free = [None, None]
        xtm_free = [None, None]
        psS_free = [None, None]
        psPT_free = [None, None]
        psO_free = [None, None]
        frees = {}
        for th in range(2):
            t0 = th * 768
            ev_last = None
            for tt in range(6):
                b = tt % 2
                if th == 0 and tt < 2:
                    tok_ld = xpre[tt]
                else:
                    tok_ld = OP("G", lambda g, tt=tt, b=b, t0=t0: g.dma_start(
                        out=xtm[b][:, :],
                        in_=dsl(x_in, t0 + tt * 128, t0 + tt * 128 + 128, 0, DM, DM)),
                        deps=[xtm_free[b]], dma="dG")
                tp_toks = [None, None]
                for i in range(NCH):
                    pb = i % 2
                    tok_tp = OP("T", lambda t, i=i, b=b, pb=pb: t.transpose(
                        psap(psPT[pb], (i // 2) * 128, [[1, 128]]),
                        xtm[b][:, i * 128:(i + 1) * 128], idb[:, :]),
                        deps=[tok_ld, t_id, psPT_free[pb]], tick=(i >= 14))
                    if i >= 14:
                        tp_toks[pb] = tok_tp
                for pb in range(2):
                    tok_ev = OP("V", lambda v, pb=pb, tt=tt: v.tensor_copy(
                        bap(REG + pb * 8 * 768 + tt * 128, [[768, 8], [1, 128]]),
                        psap(psPT[pb], 0, [[128, 8], [1, 128]])),
                        deps=[tp_toks[pb]], tick=True)
                    psPT_free[pb] = tok_ev
                    ev_last = tok_ev
                xtm_free[b] = ev_last
            x_rdy = ev_last

            for o in range(NCH):
                b = o % 4
                pp = o % 2
                tok_w = OP("G", lambda g, o=o, b=b: g.dma_start(
                    out=bap(REG + 16384 + b * 2048, [[128, 16], [1, 128]]),
                    in_=bass.AP(wk_in, o * 128, [[DM, 128], [DM * 128, 16], [1, 128]])),
                    deps=[wsb4_free[b], wsb4_free[(b + 1) % 4],
                          wsb_free[b // 2]], dma="dG")
                last_mm = None
                for (c0, c1, bk) in ((0, 512, 0), (512, 768, 512)):
                    for i in range(NCH):
                        pi = (i % 2) * 8 + i // 2
                        last_mm = OP("T", lambda t, i=i, pi=pi, c0=c0, c1=c1, bk=bk, b=b, pp=pp: t.matmul(
                            psap(psS[pp], bk, [[1, c1 - c0]]),
                            wsb4[b][:, i * 128:(i + 1) * 128],
                            xfm[pi][:, c0:c1],
                            start=(i == 0), stop=(i == NCH - 1)),
                            deps=[tok_w, x_rdy, psS_free[pp]],
                            tick=(i == NCH - 1))
                wsb4_free[b] = last_mm
                tok_ev = OP("S", lambda s, o=o, th=th, pp=pp: s.activation(
                    kT[o][:, th * 768:(th + 1) * 768],
                    psap(psS[pp], 0, [[1, 768]]), ACT.Copy),
                    deps=[last_mm], tick=True)
                psS_free[pp] = tok_ev
                if th == 1:
                    kT_done.append(tok_ev)

            qt0 = max(512, t0)
            qw = t0 + 768 - qt0
            for o in range(NCH):
                b = o % 4
                tok_w = OP("G", lambda g, o=o, b=b: g.dma_start(
                    out=bap(REG + 16384 + b * 2048, [[128, 16], [1, 128]]),
                    in_=bass.AP(wq_in, o * 128, [[DM, 128], [DM * 128, 16], [1, 128]])),
                    deps=[wsb4_free[b], wsb4_free[(b + 1) % 4]], dma="dG")
                last_mm = None
                rngs = ((0, 256, 0),) if qw == 256 else ((0, 512, 0), (512, 768, 512))
                pp = o % 2
                for (c0, c1, bk) in rngs:
                    for i in range(NCH):
                        pi = (i % 2) * 8 + i // 2
                        xc0 = qt0 - t0 + c0
                        last_mm = OP("T", lambda t, i=i, pi=pi, c0=c0, c1=c1, bk=bk, b=b, xc0=xc0, pp=pp: t.matmul(
                            psap(psS[pp], bk, [[1, c1 - c0]]),
                            wsb4[b][:, i * 128:(i + 1) * 128],
                            xfm[pi][:, xc0:xc0 + (c1 - c0)],
                            start=(i == 0), stop=(i == NCH - 1)),
                            deps=[tok_w, x_rdy, psS_free[pp]],
                            tick=(i == NCH - 1))
                wsb4_free[b] = last_mm
                qb_ = o % 2
                tok_ev = OP("S", lambda s, qw=qw, qb_=qb_, pp=pp: s.activation(
                    qst[qb_][:, 0:qw], psap(psS[pp], 0, [[1, qw]]),
                    ACT.Copy, scale=SCALE), deps=[last_mm, qst_free[qb_]], tick=True)
                psS_free[pp] = tok_ev
                tok_st = OP("Y", lambda y, o=o, qt0=qt0, qw=qw, qb_=qb_: y.dma_start(
                    out=dsl(q_dram, o * 128, o * 128 + 128, qt0 - 512, qt0 - 512 + qw, TOWN),
                    in_=qst[qb_][:, 0:qw]), deps=[tok_ev], dma="dY")
                qst_free[qb_] = tok_st
                if th == 1:
                    q_done.append(tok_st)

            for og in range(8):
                b = og % 2
                tok_w = OP("G", lambda g, og=og, b=b: g.dma_start(
                    out=bap(REG + 16384 + b * 4096, [[256, 16], [1, 256]]),
                    in_=bass.AP(wv_in, og * 256, [[DM, 128], [DM * 128, 16], [1, 256]])),
                    deps=[wsb_free[b]] + list(wsb4_free), dma="dG")
                last_mm = None
                for tt in range(6):
                    pb = tt % 2
                    for i in range(NCH):
                        pi = (i % 2) * 8 + i // 2
                        last_mm = OP("T", lambda t, i=i, pi=pi, tt=tt, b=b, pb=pb: t.matmul(
                            psap(psO[pb], 0, [[1, 256]]),
                            xfm[pi][:, tt * 128:(tt + 1) * 128],
                            wsb[b][:, i * 256:(i + 1) * 256],
                            start=(i == 0), stop=(i == NCH - 1)),
                            deps=[tok_w, x_rdy, psO_free[pb]],
                            tick=(i == NCH - 1))
                    jj = th * 6 + tt
                    tok_ev = OP("V", lambda v, jj=jj, og=og, pb=pb: v.tensor_copy(
                        vT[jj][:, og * 256:(og + 1) * 256],
                        psap(psO[pb], 0, [[1, 256]])),
                        deps=[last_mm], tick=True)
                    psO_free[pb] = tok_ev
                    if og == 7:
                        vT_done.append(tok_ev)
                wsb_free[b] = last_mm
        frees["psS"] = psS_free
        frees["psPT"] = psPT_free
        frees["psO"] = psO_free
        return frees

    fA = phase_a()
    bias0, bias0_tf = build_bias_half(0)

    attn_done = []

    def attention():
        qb_free = [None, None]
        psS_free = list(fA["psS"])
        psPT_free = list(fA["psPT"])
        psO_free = list(fA["psO"])
        pb_free = [None] * 8
        ptsb_free = [None, None]
        b1 = {"hh": 0, "prev": bias0_tf, "toks": []}
        stt_last = [None] * 16

        def emit_b1_heads(k):
            for _ in range(k):
                if b1["hh"] >= 16:
                    return
                hh = b1["hh"]
                tg2, b1["prev"] = emit_bias_head(
                    16 + hh, bneg[hh], b1["prev"], extra_deps=[stt_last[hh]])
                b1["toks"].append(tg2)
                b1["hh"] = hh + 1
        btoks = bias0
        for half in range(2):
            if half == 1:
                emit_b1_heads(16)
                btoks = b1["toks"]
            for n in range(2):
                for qc in range(4):
                    w0 = n * 512 + qc * 128
                    qbi = (n * 4 + qc) % 2
                    tok_q = OP("Y", lambda y, w0=w0, qbi=qbi: y.dma_start(
                        out=bap(QB0 + qbi * 2048, [[128, 16], [1, 128]]),
                        in_=bass.AP(q_dram, w0, [[TOWN, 128], [TOWN * 128, 16], [1, 128]])),
                        deps=q_done + [qb_free[qbi], vT_done[-1]], dma="dY")
                    last_pv = None
                    for hh in range(16):
                        h = 16 * half + hh
                        sb = hh % 2
                        kch = h // 2
                        krs = (h % 2) * 64
                        tp = (64, 0) if krs else None
                        mm_deps = [tok_q, psS_free[sb], btoks[hh], t_ones, t_mrb] + \
                            (kT_done if (half, n, qc) == (0, 0, 0) else [])

                        def s_mm(t, kch=kch, krs=krs, qbi=qbi, w0=w0, sb=sb, tp=tp, n=n, qc=qc):
                            lhs = bap(QB0 + qbi * 2048 + kch * 128, [[1, 128]],
                                      row0=krs, nrows=64)
                            t.matmul(psap(psS[sb], 0, [[1, 512]]), lhs,
                                     kT[kch][krs:krs + 64, w0:w0 + 512],
                                     start=True, stop=(n == 1), tile_position=tp)
                            if n == 0:
                                t.matmul(psap(psS[sb], 0, [[1, 512]]),
                                         onesb[0:1, :],
                                         mrowb[0:1, qc * 128:qc * 128 + 512],
                                         start=False, stop=True)
                            return t.matmul(psap(psS[sb], 512, [[1, 128]]), lhs,
                                            kT[kch][krs:krs + 64, w0 + 512:w0 + 640],
                                            start=True, stop=True, tile_position=tp)
                        tok_s = OP("T", s_mm, deps=mm_deps, tick=True)
                        tok_t = OP("V", lambda v, sb=sb, hh=hh: v.scalar_tensor_tensor(
                            out=psap(psS[sb], 0, [[1, 640]]),
                            in0=psap(psS[sb], 0, [[1, 640]]),
                            scalar=-1.0, in1=bneg[hh][:, :],
                            op0=ALU.mult, op1=ALU.add), deps=[tok_s], tick=True)
                        if half == 0 and (n, qc) == (1, 3):
                            stt_last[hh] = tok_t
                        k = hh % 8
                        tok_e = OP("S", lambda s, sb=sb, h=h, k=k: s.activation(
                            pbuf[k][:, :], psap(psS[sb], 0, [[1, 640]]),
                            ACT.Exp, scale=-1.0,
                            accum_out=dsum[:, h:h + 1]),
                            deps=[tok_t, pb_free[k]], tick=True)
                        psS_free[sb] = tok_e

                        if hh % 8 == 7:
                            h0 = h - 7
                            gs = slice(h0, h0 + 8)
                            tok_c = OP("V", lambda v, gs=gs: v.tensor_add(
                                denv[:, gs], dsum[:, gs], esnkb[:, gs]),
                                deps=[tok_e, t_esnk], tick=True)
                            tok_d = OP("V", lambda v, gs=gs: (v.drain(), v.reciprocal(
                                rdv[:, gs], denv[:, gs]), v.drain())[-1],
                                deps=[tok_c], tick=True)
                            for hh2 in range(hh - 7, hh + 1):
                                h2 = 16 * half + hh2
                                k2 = hh2 % 8
                                tok_sc = OP("V", lambda v, k2=k2, h2=h2: v.tensor_scalar_mul(
                                    pbuf[k2][:, :], pbuf[k2][:, :], rdv[:, h2:h2 + 1]),
                                    deps=[tok_d, pb_free[k2]], tick=True)
                                ptb = hh2 % 2
                                tok_tp = None
                                for i5 in range(5):
                                    tok_tp = OP("T", lambda t, k2=k2, i5=i5, ptb=ptb: t.transpose(
                                        psap(psPT[ptb], i5 * 128, [[1, 128]]),
                                        pbuf[k2][:, i5 * 128:(i5 + 1) * 128],
                                        idb[:, :]),
                                        deps=[tok_sc, psPT_free[ptb], t_id],
                                        tick=(i5 == 4))
                                pb_free[k2] = tok_tp
                                tok_pte = OP("V", lambda v, ptb=ptb: v.tensor_copy(
                                    ptsb[ptb][:, :], psap(psPT[ptb], 0, [[1, 640]])),
                                    deps=[tok_tp, ptsb_free[ptb]], tick=True)
                                psPT_free[ptb] = tok_pte
                                pair = h2 // 2
                                ob = (pair % 8) // 4
                                slot = pair % 4
                                orow = (h2 % 2) * 64
                                pv_deps = [tok_pte, psO_free[ob]] + \
                                    (vT_done if (half, n, qc) == (0, 0, 0) else [])

                                def pv(t, h2=h2, n=n, qc=qc, slot=slot, orow=orow,
                                       ob=ob, ptb=ptb):
                                    ins = None
                                    for i5 in range(5):
                                        vch = n * 4 + qc + i5
                                        ins = t.matmul(
                                            psap(psO[ob], slot * 128, [[1, 128]],
                                                 row0=orow, nrows=64),
                                            vT[vch][:, h2 * 64:(h2 + 1) * 64],
                                            ptsb[ptb][:, i5 * 128:(i5 + 1) * 128],
                                            start=(i5 == 0), stop=(i5 == 4),
                                            tile_position=((0, 64) if orow else None))
                                    return ins
                                tok_pv = OP("T", pv, deps=pv_deps, tick=True)
                                ptsb_free[ptb] = tok_pv
                                last_pv = tok_pv
                                if h2 % 8 == 7:
                                    pg = h2 // 8

                                    def oev(se, pg=pg, n=n, qc=qc, ob=ob):
                                        return se.activation(
                                            bap(AFM0 + n * 8192 + pg * 2048 + qc * 128,
                                                [[512, 4], [1, 128]]),
                                            psap(psO[ob], 0, [[128, 4], [1, 128]]),
                                            ACT.Copy)
                                    tok_oe = OP("S", oev, deps=[tok_pv], tick=True)
                                    psO_free[ob] = tok_oe
                                    if half == 1:
                                        attn_done.append(tok_oe)
                    qb_free[qbi] = last_pv
                    if half == 0 and (n, qc) == (1, 3):
                        emit_b1_heads(16)
        return {"psS": psS_free, "psPT": psPT_free, "psO": psO_free,
                "ptsb": ptsb_free}

    fAT = attention()

    if stop_after == "attn":
        toks = attn_done
        d_pb = nc.dram_tensor("d_pb", [128, 5120], BF16, kind="ExternalOutput")
        d_pt = nc.dram_tensor("d_pt", [128, 1280], BF16, kind="ExternalOutput")
        t1 = OP("Y", lambda y: y.dma_start(out=d_pb.ap(), in_=bap(REG, [[1, 5120]])),
                deps=toks + [kb.cnt and None], dma="dY")
        t2 = OP("Y", lambda y: y.dma_start(out=d_pt.ap(), in_=bap(REG + 5120, [[1, 1280]])),
                deps=toks, dma="dY")
        if dbg:
            for nm_, tgt, srcap in [
                ("d_bneg", d_bneg, fap(BN0, [[1, 10240]])),
                ("d_afm", d_afm, bap(AFM0, [[1, 16384]])),
                ("d_stat", d_stat, fap(ST0, [[1, 688]])),
            ]:
                OP("Y", lambda y, tgt=tgt, srcap=srcap: y.dma_start(
                    out=tgt.ap(), in_=srcap), deps=toks, dma="dY")
        _afins = kb.dma_total("dY")
        def f_afin(y, fins=_afins):
            ins = None
            for s_, v_ in fins:
                ins = y.wait_ge(sems[s_], v_)
            return ins
        OP("Y", f_afin)
        with nc.Block() as block:
            kb.emit(block, sems)
        ctx.close()
        return nc

    y1_done = []
    yfm_done = []

    def ln_block(r1_deps, sq_free, y_free, out_tile, rb, so):
        tok_sum = OP("V", lambda v: v.reduce_sum(lnst[:, so:so + 1], rb[:, :], axis=AX),
                     deps=r1_deps, tick=True)
        tok_sq = OP("S", lambda s: s.activation(sqb[:, :], rb[:, :], ACT.Square,
                                                accum_out=lnst[:, so + 1:so + 2]),
                    deps=r1_deps + [sq_free], tick=True)
        tok_mean = OP("V", lambda v: v.tensor_scalar_mul(lnst[:, so + 2:so + 3],
                                                         lnst[:, so:so + 1],
                                                         1.0 / DM),
                      deps=[tok_sum], tick=True)
        tok_msq = OP("V", lambda v: v.tensor_mul(lnst[:, so + 3:so + 4],
                                                 lnst[:, so + 2:so + 3],
                                                 lnst[:, so + 2:so + 3]),
                     deps=[tok_mean], tick=True)
        tok_var = OP("V", lambda v: v.scalar_tensor_tensor(
            out=lnst[:, so + 4:so + 5], in0=lnst[:, so + 1:so + 2], scalar=1.0 / DM,
            in1=lnst[:, so + 3:so + 4], op0=ALU.mult, op1=ALU.subtract),
            deps=[tok_sq, tok_msq], tick=True)
        tok_vare = OP("V", lambda v: v.tensor_scalar_add(lnst[:, so + 4:so + 5],
                                                         lnst[:, so + 4:so + 5],
                                                         EPS), deps=[tok_var], tick=True)
        tok_ln = OP("S", lambda s: s.activation(lnst[:, so + 5:so + 6],
                                                lnst[:, so + 4:so + 5], ACT.Ln),
                    deps=[tok_vare], tick=True)
        tok_rs = OP("S", lambda s: s.activation(lnst[:, so + 6:so + 7],
                                                lnst[:, so + 5:so + 6], ACT.Exp,
                                                scale=-0.5), deps=[tok_ln], tick=True)
        tok_nmr = OP("V", lambda v: v.tensor_scalar(lnst[:, so + 7:so + 8],
                                                    lnst[:, so + 2:so + 3],
                                                    lnst[:, so + 6:so + 7], -1.0,
                                                    ALU.mult, ALU.mult),
                     deps=[tok_rs], tick=True)
        tok_ap = OP("S", lambda s: s.activation(out_tile[:, :], rb[:, :], ACT.Identity,
                                                bias=lnst[:, so + 7:so + 8],
                                                scale=lnst[:, so + 6:so + 7]),
                    deps=[tok_nmr] + list(y_free), tick=True)
        return tok_ap, tok_sq

    def wo_r1_ln1():
        wsb_free = [None, None]
        psO_free = list(fAT["psO"])
        wo_ev = {}
        for o in range(NCH):
            b = o % 2
            tok_w = OP("G", lambda g, o=o, b=b: g.dma_start(
                out=bap(REG + 6400 + b * 2048, [[128, 16], [1, 128]]),
                in_=bass.AP(wo_in, o * 128, [[DM, 128], [DM * 128, 16], [1, 128]])),
                deps=[wsb_free[b]] + (attn_done if o < 2 else []), dma="dG")
            last_mm = None
            for n in range(2):
                pb = (o * 2 + n) % 2
                for pair in range(16):
                    last_mm = OP("T", lambda t, pair=pair, n=n, b=b, pb=pb: t.matmul(
                        psap(psO[pb], 0, [[1, 512]]),
                        wsb2[b][:, pair * 128:(pair + 1) * 128],
                        bap(AFM0 + n * 8192 + pair * 512, [[1, 512]]),
                        start=(pair == 0), stop=(pair == 15)),
                        deps=[tok_w, psO_free[pb]] + attn_done,
                        tick=(pair == 15))
                tok_ev = OP("S", lambda s, o=o, n=n, pb=pb: s.activation(
                    wofm[0:128, o * 1024 + n * 512:o * 1024 + n * 512 + 512],
                    psap(psO[pb], 0, [[1, 512]]), ACT.Copy),
                    deps=[last_mm], tick=True)
                psO_free[pb] = tok_ev
                wo_ev[(o, n)] = tok_ev
            wsb_free[b] = last_mm

        xb_free = [attn_done[-1], attn_done[-1]]
        psPT_free = list(fAT["psPT"])
        r1_free = [None, None]
        y1_free = []
        sq_free = fAT["psS"][0]
        for n in range(2):
            for ts_ in range(4):
                gt = n * 512 + ts_ * 128
                bx = (n * 4 + ts_) % 2
                tok_x = OP("Y", lambda y, gt=gt, bx=bx: y.dma_start(
                    out=xb[bx][:, :],
                    in_=dsl(x_in, 512 + gt, 512 + gt + 128, 0, DM, DM)),
                    deps=[xb_free[bx]], dma="dY")
                tpt = [None, None]
                for o in range(NCH):
                    tok_tp = OP("T", lambda t, o=o, n=n, ts_=ts_: t.transpose(
                        psap(psPT[o // 8], (o % 8) * 128, [[1, 128]]),
                        wofm[0:128, o * 1024 + n * 512 + ts_ * 128:
                             o * 1024 + n * 512 + ts_ * 128 + 128],
                        idb[:, :]),
                        deps=[wo_ev[(o, n)], t_id, psPT_free[o // 8]],
                        tick=(o % 8 == 7))
                    if o % 8 == 7:
                        tpt[o // 8] = tok_tp
                rb = r1b if (n * 4 + ts_) % 2 == 0 else r1c
                so = 16 * ((n * 4 + ts_) % 2)
                tok_r1a = OP("V", lambda v, bx=bx, rb=rb: v.tensor_add(
                    rb[:, 0:1024], psap(psPT[0], 0, [[1, 1024]]),
                    xb[bx][:, 0:1024]), deps=[tpt[0], tok_x, r1_free[(n * 4 + ts_) % 2]], tick=True)
                tok_r1b = OP("V", lambda v, bx=bx, rb=rb: v.tensor_add(
                    rb[:, 1024:2048], psap(psPT[1], 0, [[1, 1024]]),
                    xb[bx][:, 1024:2048]), deps=[tpt[1], tok_x], tick=True)
                psPT_free[0] = tok_r1a
                psPT_free[1] = tok_r1b
                xb_free[bx] = tok_r1b
                tok_y1, tok_sq = ln_block([tok_r1a, tok_r1b], sq_free, y1_free,
                                          y1b, rb, so)
                sq_free = tok_sq
                r1_free[(n * 4 + ts_) % 2] = tok_y1
                tok_st = OP("Y", lambda y, gt=gt: y.dma_start(
                    out=dsl(y1_dram, gt, gt + 128, 0, DM, DM), in_=y1b[:, :]),
                    deps=[tok_y1], dma="dY")
                y1_done.append(tok_st)
                tok_ybf = OP("V", lambda v: v.tensor_copy(sqb[:, :], y1b[:, :]),
                             deps=[tok_y1, tok_sq], tick=True)
                sq_free = tok_ybf
                tpt2 = [None, None]
                for i in range(NCH):
                    tok_tp = OP("T", lambda t, i=i: t.transpose(
                        psap(psPT[i // 8], (i % 8) * 128, [[1, 128]]),
                        sqb[:, i * 128:(i + 1) * 128], idb[:, :]),
                        deps=[tok_ybf, t_id, psPT_free[i // 8]], tick=(i % 8 == 7))
                    if i % 8 == 7:
                        tpt2[i // 8] = tok_tp
                for pb in range(2):
                    tok_ev = OP("V", lambda v, pb=pb, n=n, ts_=ts_: v.tensor_copy(
                        bap(YFM0 + pb * 8 * 1024 + n * 512 + ts_ * 128,
                            [[1024, 8], [1, 128]]),
                        psap(psPT[pb], 0, [[128, 8], [1, 128]])),
                        deps=[tpt2[pb]], tick=True)
                    psPT_free[pb] = tok_ev
                    yfm_done.append(tok_ev)
                y1_free = [tok_st, tok_ybf]
        return {"psO": psO_free, "psPT": psPT_free, "sq": sq_free,
                "xb": xb_free, "y1": y1_free, "r1": r1_free[1] or r1_free[0],
                "psS": fAT["psS"], "wsb": wsb_free}

    fWO = wo_r1_ln1()

    if stop_after == "wo":
        toks = [fWO["r1"], y1_done[-1], yfm_done[-1]]
        for nm_, tgt, srcap in [
            ("d_afm", d_afm, bap(AFM0, [[1, 16384]])),
            ("d_wofm", d_wofm, bap(0, [[1, 16384]])),
            ("d_yfm", d_yfm, bap(YFM0, [[1, 16384]])),
        ]:
            OP("Y", lambda y, tgt=tgt, srcap=srcap: y.dma_start(
                out=tgt.ap(), in_=srcap), deps=toks, dma="dY")
        _wfins = kb.dma_total("dY")
        def f_wfin(y, fins=_wfins):
            ins = None
            for s_, v_ in fins:
                ins = y.wait_ge(sems[s_], v_)
            return ins
        OP("Y", f_wfin)
        with nc.Block() as block:
            kb.emit(block, sems)
        ctx.close()
        return nc

    def ffn():
        guard = [attn_done[-1]]
        tok_at = OP("G", lambda g: g.dma_start(
            out=bap(REG, [[128, 16], [1, 128]]),
            in_=bass.AP(at_in, 0, [[128, 128], [128 * 128, 16], [1, 128]])),
            deps=guard, dma="dG")
        tok_bt = OP("G", lambda g: g.dma_start(out=btb[:, :], in_=bt_in.ap()),
                    deps=guard, dma="dG")
        tok_rw = OP("G", lambda g: g.dma_start(
            out=bap(RW0, [[8, 16], [1, 8]]),
            in_=bass.AP(rwt_in, 0, [[8, 128], [1024, 16], [1, 8]])),
            deps=guard, dma="dG")

        psO_free = list(fWO["psO"])
        psS_free = list(fWO["psS"])
        psPT_free = list(fWO["psPT"])
        scax_tok = [fWO["r1"], fWO["r1"]]
        for n in range(2):
            last_mm = None
            for i in range(NCH):
                last_mm = OP("T", lambda t, i=i, n=n: t.matmul(
                    psap(psO[0], 0, [[1, 512]], nrows=8),
                    rwb[:, i * 8:(i + 1) * 8],
                    yfm[i][:, n * 512:n * 512 + 512],
                    start=(i == 0), stop=(i == NCH - 1)),
                    deps=[tok_rw, psO_free[0]] + yfm_done, tick=(i == NCH - 1))
            tok_lge = OP("V", lambda v: v.tensor_copy(
                lgfm[0:8, :], psap(psO[0], 0, [[1, 512]], nrows=8)),
                deps=[last_mm], tick=True)
            psO_free[0] = tok_lge
            last_ax = None
            for i in range(NCH):
                last_ax = OP("T", lambda t, i=i, n=n: t.matmul(
                    psap(psO[1], 0, [[1, 512]]),
                    atb[:, i * 128:(i + 1) * 128],
                    yfm[i][:, n * 512:n * 512 + 512],
                    start=(i == 0), stop=(i == NCH - 1)),
                    deps=[tok_at, psO_free[1]] + yfm_done, tick=(i == NCH - 1))
            wfm_toks = []
            for q in range(4):
                tok_tp = OP("T", lambda t, q=q: t.transpose(
                    psap(psS[0], 0, [[1, 8]]),
                    lgfm[0:8, q * 128:(q + 1) * 128], idb32[0:8, :]),
                    deps=[tok_lge, t_id32, psS_free[0]], tick=True)
                tok_lgt = OP("V", lambda v: v.tensor_copy(
                    lgtm[:, 0:8], psap(psS[0], 0, [[1, 8]])),
                    deps=[tok_tp], tick=True)
                psS_free[0] = tok_lgt
                tok_m1 = OP("V", lambda v: v.reduce_max(rst[:, 0:1], lgtm[:, 0:8],
                                                        axis=AX), deps=[tok_lgt], tick=True)
                tok_nm1 = OP("V", lambda v: v.reduce_max(rst[:, 1:2], lgtm[:, 0:8],
                                                         axis=AX, negate=True),
                             deps=[tok_lgt], tick=True)
                tok_ind = OP("V", lambda v: v.tensor_single_scalar(
                    rst[:, 8:16], lgtm[:, 0:8], rst[:, 0:1], ALU.is_ge),
                    deps=[tok_m1], tick=True)
                tok_msk = OP("V", lambda v: v.scalar_tensor_tensor(
                    out=rst[:, 16:24], in0=rst[:, 8:16], scalar=NEG,
                    in1=lgtm[:, 0:8], op0=ALU.mult, op1=ALU.add),
                    deps=[tok_ind], tick=True)
                tok_m2 = OP("V", lambda v: v.reduce_max(rst[:, 2:3], rst[:, 16:24],
                                                        axis=AX), deps=[tok_msk], tick=True)
                tok_ind2 = OP("V", lambda v: v.tensor_single_scalar(
                    rst[:, 8:16], lgtm[:, 0:8], rst[:, 2:3], ALU.is_ge),
                    deps=[tok_m2], tick=True)
                tok_u = OP("S", lambda s: s.activation(
                    rst[:, 24:32], lgtm[:, 0:8], ACT.Exp, bias=rst[:, 1:2]),
                    deps=[tok_nm1, tok_msk], tick=True)
                tok_s2 = OP("S", lambda s: s.activation(
                    rst[:, 3:4], rst[:, 2:3], ACT.Exp, bias=rst[:, 1:2]),
                    deps=[tok_m2], tick=True)
                tok_nd = OP("V", lambda v: v.tensor_scalar_add(rst[:, 4:5], rst[:, 3:4],
                                                               1.0), deps=[tok_s2], tick=True)
                tok_rn = OP("V", lambda v: (v.drain(), v.reciprocal(rst[:, 5:6], rst[:, 4:5]),
                                            v.drain())[-1], deps=[tok_nd], tick=True)
                tok_w1 = OP("V", lambda v: v.tensor_mul(rst[:, 8:16], rst[:, 8:16],
                                                        rst[:, 24:32]),
                            deps=[tok_u, tok_ind2], tick=True)
                tok_wv = OP("V", lambda v: (v.drain(), v.tensor_scalar(
                    wtm[:, 0:8], rst[:, 8:16], rst[:, 5:6], 2.0, ALU.mult,
                    ALU.mult))[-1], deps=[tok_w1, tok_rn], tick=True)
                tok_wt = OP("T", lambda t: t.transpose(
                    psap(psPT[0], 0, [[1, 128]], nrows=8),
                    wtm[:, 0:8], idb[:, :]),
                    deps=[tok_wv, psPT_free[0], t_id], tick=True)
                tok_we = OP("V", lambda v, q=q: v.tensor_copy(
                    wfm[0:8, q * 128:(q + 1) * 128],
                    psap(psPT[0], 0, [[1, 128]], nrows=8)),
                    deps=[tok_wt], tick=True)
                psPT_free[0] = tok_we
                wfm_toks.append(tok_we)
            tok_wr = OP("T", lambda t: t.matmul(
                psap(psS[0], 0, [[1, 512]]),
                e8b[0:8, :], wfm[0:8, :], start=True, stop=True),
                deps=wfm_toks + [t_e8, psS_free[0]], tick=True)
            tok_wre = OP("V", lambda v: v.tensor_copy(
                wrepf[:, :], psap(psS[0], 0, [[1, 512]])),
                deps=[tok_wr], tick=True)
            psS_free[0] = tok_wre
            tok_sc = OP("V", lambda v, n=n: v.tensor_mul(
                scax[n][:, :], psap(psO[1], 0, [[1, 512]]), wrepf[:, :]),
                deps=[last_ax, tok_wre, scax_tok[n]], tick=True)
            psO_free[1] = tok_sc
            scax_tok[n] = tok_sc

        wsb_free = list(fWO["wsb"])
        psO_free2 = [psO_free[0], psO_free[1]]
        ffm_ev = {}
        for o in range(NCH):
            b = o % 2
            tok_w = OP("G", lambda g, o=o, b=b: g.dma_start(
                out=bap(REG + 6400 + b * 2048, [[128, 16], [1, 128]]),
                in_=bass.AP(wt_in, o * 128, [[DM, 128], [DM * 128, 16], [1, 128]])),
                deps=[wsb_free[b]] + (list(fWO["y1"]) + [yfm_done[-1], fWO["sq"]]
                                      if o < 2 else []),
                dma="dG")
            last = None
            for n in range(2):
                pb = (o * 2 + n) % 2
                for i in range(NCH):
                    OP("T", lambda t, i=i, n=n, b=b, pb=pb: t.matmul(
                        psap(psO[pb], 0, [[1, 512]]),
                        wsb2[b][:, i * 128:(i + 1) * 128],
                        yfm[i][:, n * 512:n * 512 + 512],
                        start=(i == 0), stop=False),
                        deps=[tok_w, psO_free2[pb]] + yfm_done, tick=False)
                last = OP("T", lambda t, o=o, n=n, pb=pb: t.matmul(
                    psap(psO[pb], 0, [[1, 512]]),
                    btb[:, o * 128:(o + 1) * 128],
                    scax[n][:, :], start=False, stop=True),
                    deps=[tok_bt, scax_tok[n]], tick=True)
                tok_ev = OP("S", lambda s, o=o, n=n, pb=pb: s.activation(
                    ffm[0:128, o * 1024 + n * 512:o * 1024 + n * 512 + 512],
                    psap(psO[pb], 0, [[1, 512]]), ACT.Copy),
                    deps=[last], tick=True)
                psO_free2[pb] = tok_ev
                ffm_ev[(o, n)] = tok_ev
            wsb_free[b] = last

        yb_free = list(fWO["xb"])
        r2_free = [fWO["r1"], fWO["r1"]]
        o_free = [y1_done[-1]] + fWO["y1"]
        sq_free = fWO["sq"]
        out_toks = []
        for n in range(2):
            for ts_ in range(4):
                gt = n * 512 + ts_ * 128
                bx = (n * 4 + ts_) % 2
                tok_yl = OP("Y", lambda y, gt=gt, bx=bx: y.dma_start(
                    out=xb[bx][:, :], in_=dsl(y1_dram, gt, gt + 128, 0, DM, DM)),
                    deps=[yb_free[bx]] + y1_done, dma="dY")
                tpt = [None, None]
                for o in range(NCH):
                    tok_tp = OP("T", lambda t, o=o, n=n, ts_=ts_: t.transpose(
                        psap(psPT[o // 8], (o % 8) * 128, [[1, 128]]),
                        ffm[0:128, o * 1024 + n * 512 + ts_ * 128:
                            o * 1024 + n * 512 + ts_ * 128 + 128],
                        idb[:, :]),
                        deps=[ffm_ev[(o, n)], t_id, psPT_free[o // 8]],
                        tick=(o % 8 == 7))
                    if o % 8 == 7:
                        tpt[o // 8] = tok_tp
                rb = r1b if (n * 4 + ts_) % 2 == 0 else r1c
                so = 16 * ((n * 4 + ts_) % 2)
                tok_r2a = OP("V", lambda v, bx=bx, rb=rb: v.tensor_add(
                    rb[:, 0:1024], psap(psPT[0], 0, [[1, 1024]]),
                    xb[bx][:, 0:1024]), deps=[tpt[0], tok_yl, r2_free[(n * 4 + ts_) % 2]], tick=True)
                tok_r2b = OP("V", lambda v, bx=bx, rb=rb: v.tensor_add(
                    rb[:, 1024:2048], psap(psPT[1], 0, [[1, 1024]]),
                    xb[bx][:, 1024:2048]), deps=[tpt[1], tok_yl], tick=True)
                psPT_free[0] = tok_r2a
                psPT_free[1] = tok_r2b
                yb_free[bx] = tok_r2b
                tok_o, tok_sq = ln_block([tok_r2a, tok_r2b], sq_free, o_free,
                                         y1b, rb, so)
                sq_free = tok_sq
                r2_free[(n * 4 + ts_) % 2] = tok_o
                tok_st = OP("Y", lambda y, gt=gt: y.dma_start(
                    out=dsl(y_out, gt, gt + 128, 0, DM, DM), in_=y1b[:, :]),
                    deps=[tok_o], dma="dY")
                o_free = [tok_st]
                out_toks.append(tok_st)
        _fins = kb.dma_total("dY")
        def f_fin(y, fins=_fins):
            ins = None
            for s_, v_ in fins:
                ins = y.wait_ge(sems[s_], v_)
            return ins
        OP("Y", f_fin, deps=out_toks)

    ffn()

    if dbg:
        for nm_, src, dep in [
            ("d_bneg", [fap(BN0, [[1, 8192]])], None),
            ("d_afm", [bap(AFM0, [[1, 16384]])], None),
            ("d_wofm", [bap(0, [[1, 16384]])], None),
            ("d_yfm", [bap(YFM0, [[1, 16384]])], None),
            ("d_ffm", [bap(FFM0, [[1, 16384]])], None),
            ("d_stat", [fap(ST0, [[1, 688]])], None),
            ("d_scax", [bap(REG + 4224, [[1, 1024]])], None),
        ]:
            tgt = {"d_bneg": d_bneg, "d_afm": d_afm, "d_wofm": d_wofm,
                   "d_yfm": d_yfm, "d_ffm": d_ffm, "d_stat": d_stat,
                   "d_scax": d_scax}[nm_]
            OP("Y", lambda y, tgt=tgt, src=src: y.dma_start(
                out=tgt.ap(), in_=src[0]), deps=[], dma="dY")
        _fins2 = kb.dma_total("dY")
        def f_fin2(y, fins=_fins2):
            ins = None
            for s_, v_ in fins:
                ins = y.wait_ge(sems[s_], v_)
            return ins
        OP("Y", f_fin2)

    with nc.Block() as block:
        kb.emit(block, sems)
    ctx.close()
    return nc


def kernel(L0_attn_x, L0_attn_Wq, L0_attn_Wk, L0_attn_Wv, L0_attn_Wo,
           L0_attn_dog_params, L0_attn_sinks, L0_ffn_weight, L0_ffn_lora_A,
           L0_ffn_lora_B, L0_ffn_router_w):
    x = np.ascontiguousarray(np.asarray(L0_attn_x, dtype=np.float32).reshape(8192, DM))
    wq = np.ascontiguousarray(np.asarray(L0_attn_Wq, dtype=np.float32))
    wk = np.ascontiguousarray(np.asarray(L0_attn_Wk, dtype=np.float32))
    wv = np.ascontiguousarray(np.asarray(L0_attn_Wv, dtype=np.float32))
    wo = np.ascontiguousarray(np.asarray(L0_attn_Wo, dtype=np.float32))
    wt = np.ascontiguousarray(np.asarray(L0_ffn_weight, dtype=np.float32).T)
    at = np.ascontiguousarray(
        np.asarray(L0_ffn_lora_A, dtype=np.float32).reshape(128, DM).T)
    bt = np.ascontiguousarray(
        np.transpose(np.asarray(L0_ffn_lora_B, dtype=np.float32), (0, 2, 1)).reshape(128, DM))
    rwt = np.ascontiguousarray(np.asarray(L0_ffn_router_w, dtype=np.float32).T)
    dog = np.ascontiguousarray(np.broadcast_to(
        np.asarray(L0_attn_dog_params, dtype=np.float32).T.reshape(1, 224), (128, 224)))
    snk = np.ascontiguousarray(np.broadcast_to(
        np.asarray(L0_attn_sinks, dtype=np.float32).reshape(1, 32), (128, 32)))

    nc = bass.Bass("TRN2", target_bir_lowering=False)
    build_program(nc)

    in_maps = []
    for c in range(NCORES):
        lo = c * TOWN - HALO
        if lo < 0:
            xs = np.concatenate([np.zeros((HALO, DM), np.float32),
                                 x[0:c * TOWN + TOWN]], axis=0)
        else:
            xs = x[lo:c * TOWN + TOWN]
        in_maps.append({
            "x": np.ascontiguousarray(xs), "wq": wq, "wk": wk, "wv": wv,
            "wo": wo, "wt": wt, "at": at, "bt": bt, "rwt": rwt, "dog": dog,
            "snk": snk, "cid": np.full((1, 4), float(c), np.float32),
        })

    res = run_bass_kernel_spmd(nc, in_maps, core_ids=list(range(NCORES)))
    y = np.concatenate([r["y"] for r in res.results], axis=0)
    return y.reshape(1, 8192, DM)


# version 23
# speedup vs baseline: 1.0124x; 1.0124x over previous
import numpy as np
import contextlib
import concourse.bass as bass
import concourse.mybir as mybir
from concourse.bass_utils import run_bass_kernel_spmd

F32 = mybir.dt.float32
BF16 = mybir.dt.bfloat16
AX = mybir.AxisListType.X
ALU = mybir.AluOpType
ACT = mybir.ActivationFunctionType

NCORES = 8
DM = 2048
H = 32
DH = 64
WIN = 512
TOWN = 1024
HALO = 512
TALL = TOWN + HALO
NCH = DM // 128
NEG = -1.0e30
POS = 1.0e30
EPS = 1e-5
SCALE = DH ** -0.5
SQ2 = float(np.sqrt(2.0))

ENGS = ("T", "V", "S", "G", "Y")
ENG2SEM = {"T": "sT", "V": "sV", "S": "sS", "G": "sG", "Y": "sY"}
DMA_SEMS = {"dY": ["dY0", "dY1", "dY2", "dY3"],
            "dG": ["dG0", "dG1", "dG2", "dG3"]}
SEMS = ["sT", "sV", "sS", "sG", "sY"] + DMA_SEMS["dY"] + DMA_SEMS["dG"]

BF_N = 77568
F32_N = 12864

KT0 = 0
VT0 = 24576
REG = 49152
STAT_BF = 76032
assert REG + 26880 <= STAT_BF and STAT_BF + 1536 <= BF_N

BN0 = 0
G0 = 10240
ST0 = 12160
DOG0 = ST0 + 256
DRV0 = DOG0 + 224
FLG0 = DRV0 + 192
ID32 = 12840
assert ID32 + 8 <= F32_N


class TV:
    def __init__(self, tensor, pitch, off, width, name=""):
        self.t, self.pitch, self.off, self.w, self.name = tensor, pitch, off, width, name

    def __getitem__(self, idx):
        rs, cs = idx
        r0 = rs.start or 0
        r1 = rs.stop if rs.stop is not None else 128
        c0 = cs.start or 0
        c1 = cs.stop if cs.stop is not None else self.w
        assert 0 <= c0 <= c1 <= self.w, (self.name, c0, c1, self.w)
        return bass.AP(self.t, r0 * self.pitch + self.off + c0,
                       [[self.pitch, r1 - r0], [1, c1 - c0]])


class KB:
    def __init__(self, nc):
        self.nc = nc
        self.qs = {e: [] for e in ENGS}
        self.cnt = {s: 0 for s in SEMS}
        self.hw = {e: {s: 0 for s in SEMS} for e in ENGS}
        self.dmai = {"dY": 0, "dG": 0}

    def dma_total(self, q):
        return [(s, self.cnt[s]) for s in DMA_SEMS[q] if self.cnt[s] > 0]

    def op(self, eng, fn, deps=(), tick=False, dma=None):
        deps = list(deps)
        inc = None
        tok = None
        if dma is not None:
            i = self.dmai[dma]
            self.dmai[dma] = i + 1
            sem = DMA_SEMS[dma][i % 4]
            if self.cnt[sem] > 0:
                deps.append((sem, self.cnt[sem]))
            self.cnt[sem] += 16
            inc = (sem, 16)
            tok = (sem, self.cnt[sem])
        elif tick:
            own = ENG2SEM[eng]
            self.cnt[own] += 1
            inc = (own, 1)
            tok = (own, self.cnt[own])
        waits = []
        own = ENG2SEM[eng]
        for d in deps:
            if d is None:
                continue
            s, v = d
            if s == own and eng in ("T", "Y"):
                continue
            if self.hw[eng][s] >= v:
                continue
            self.hw[eng][s] = v
            waits.append((s, v))
        self.qs[eng].append((fn, waits, inc))
        return tok

    def emit(self, block, sems):
        def mk(ops):
            def run(eng):
                for fn, waits, inc in ops:
                    for s, v in waits:
                        eng.wait_ge(sems[s], v)
                    ins = fn(eng)
                    if inc is not None:
                        ins.then_inc(sems[inc[0]], inc[1])
            return run
        block.tensor(mk(self.qs["T"]))
        block.vector(mk(self.qs["V"]))
        block.scalar(mk(self.qs["S"]))
        block.gpsimd(mk(self.qs["G"]))
        block.sync(mk(self.qs["Y"]))


def dsl(t, r0, r1, c0, c1, cols):
    return bass.AP(t, r0 * cols + c0, [[cols, r1 - r0], [1, c1 - c0]])


def build_program(nc, dbg=False, stop_after=None):
    x_in = nc.dram_tensor("x", [TALL, DM], F32, kind="ExternalInput")
    wq_in = nc.dram_tensor("wq", [DM, DM], F32, kind="ExternalInput")
    wk_in = nc.dram_tensor("wk", [DM, DM], F32, kind="ExternalInput")
    wv_in = nc.dram_tensor("wv", [DM, DM], F32, kind="ExternalInput")
    wo_in = nc.dram_tensor("wo", [DM, DM], F32, kind="ExternalInput")
    wt_in = nc.dram_tensor("wt", [DM, DM], F32, kind="ExternalInput")
    at_in = nc.dram_tensor("at", [DM, 128], F32, kind="ExternalInput")
    bt_in = nc.dram_tensor("bt", [128, DM], F32, kind="ExternalInput")
    rwt_in = nc.dram_tensor("rwt", [DM, 8], F32, kind="ExternalInput")
    dog_in = nc.dram_tensor("dog", [128, 224], F32, kind="ExternalInput")
    snk_in = nc.dram_tensor("snk", [128, 32], F32, kind="ExternalInput")
    cid_in = nc.dram_tensor("cid", [1, 4], F32, kind="ExternalInput")
    y_out = nc.dram_tensor("y", [TOWN, DM], F32, kind="ExternalOutput")
    dkind = "ExternalOutput" if dbg else "Internal"
    q_dram = nc.dram_tensor("qscr", [DM, TOWN], BF16, kind=dkind)
    y1_dram = nc.dram_tensor("y1scr", [TOWN, DM], F32, kind=dkind)
    if dbg:
        d_bneg = nc.dram_tensor("d_bneg", [128, 8192], F32, kind="ExternalOutput")
        d_afm = nc.dram_tensor("d_afm", [128, 16384], BF16, kind="ExternalOutput")
        d_wofm = nc.dram_tensor("d_wofm", [128, 16384], BF16, kind="ExternalOutput")
        d_yfm = nc.dram_tensor("d_yfm", [128, 16384], BF16, kind="ExternalOutput")
        d_ffm = nc.dram_tensor("d_ffm", [128, 16384], BF16, kind="ExternalOutput")
        d_stat = nc.dram_tensor("d_stat", [128, 688], F32, kind="ExternalOutput")
        d_scax = nc.dram_tensor("d_scax", [128, 1024], BF16, kind="ExternalOutput")

    abf = nc.alloc_sbuf_tensor("abf", [128, BF_N], BF16)
    af = nc.alloc_sbuf_tensor("af32", [128, F32_N], F32)

    def bv(off, w, name=""):
        return TV(abf, BF_N, off, w, name)

    def fv(off, w, name=""):
        return TV(af, F32_N, off, w, name)

    def bap(off, dims, row0=0, nrows=128):
        return bass.AP(abf, row0 * BF_N + off, [[BF_N, nrows]] + dims)

    def fap(off, dims, row0=0, nrows=128):
        return bass.AP(af, row0 * F32_N + off, [[F32_N, nrows]] + dims)

    kT = [bv(KT0 + i * TALL, TALL, f"kT{i}") for i in range(NCH)]
    vT = [bv(VT0 + j * DM, DM, f"vT{j}") for j in range(12)]
    xfm = [bv(REG + i * 768, 768, f"xfm{i}") for i in range(NCH)]
    xtm = [bv(REG + 12288 + b * DM, DM, f"xtm{b}") for b in range(2)]
    wsb = [bv(REG + 16384 + b * 4096, 4096, f"wsb{b}") for b in range(2)]
    wsb4 = [bv(REG + 16384 + j * 2048, 2048, f"wsb4_{j}") for j in range(4)]
    qst = [bv(REG + 24576 + j * 768, 768, f"qst{j}") for j in range(2)]
    pbuf = [bv(REG + k * 640, 640, f"pb{k}") for k in range(8)]
    ptsb = [bv(REG + 5120 + b * 640, 640, f"ptsb{b}") for b in range(2)]
    QB0 = REG + 6400
    AFM0 = REG + 10496
    afm = bv(AFM0, 16384, "afm")
    wofm = bv(0, 16384, "wofm")
    yfm = [bv(16384 + i * 1024, 1024, f"yfm{i}") for i in range(NCH)]
    YFM0 = 16384
    ffm = bv(32768, 16384, "ffm")
    FFM0 = 32768
    atb = bv(REG, DM, "atb")
    btb = bv(REG + 2048, DM, "btb")
    rwb = bv(REG + 4096, 128, "rwb")
    RW0 = REG + 4096
    scax = [bv(REG + 4224 + b * 512, 512, f"scax{b}") for b in range(2)]
    wsb2 = [bv(REG + 6400 + b * 2048, 2048, f"wsb2{b}") for b in range(2)]
    sqb = bv(REG + 6400, DM, "sqb")
    idb = bv(STAT_BF, 128, "idb")
    e8b = bv(STAT_BF + 128, 128, "e8")
    onesb = bv(STAT_BF + 256, 128, "ones")
    mrowb = bv(STAT_BF + 384, 1152, "mrow")

    bneg = [fv(BN0 + k * 640, 640, f"bneg{k}") for k in range(16)]
    xb = [fv(b * 2048, DM, f"xb{b}") for b in range(2)]
    r1b = fv(4096, DM, "r1b")
    r1c = fv(8192, DM, "r1c")
    y1b = fv(6144, DM, "y1b")
    gdist = fv(G0, 640, "gdist")
    gscr = fv(G0 + 640, 640, "gscr")
    gscr2 = fv(G0 + 1280, 640, "gscr2")
    lgfm = fv(G0, 512, "lgfm")
    lgtm = fv(G0 + 512, 32, "lgtm")
    rst = fv(G0 + 552, 64, "rst")
    wtm = bv(REG + 5760, 8, "wtm")
    wfm = bv(REG + 5248, 512, "wfm")
    wrepf = fv(G0 + 1136, 512, "wrepf")
    nmv = fv(ST0, 32, "nm")
    nm2v = fv(ST0 + 32, 32, "nm2")
    dsum = fv(ST0 + 64, 32, "dsum")
    tmp32 = fv(ST0 + 96, 32, "tmp32")
    denv = fv(ST0 + 128, 32, "den")
    rdv = fv(ST0 + 160, 32, "rd")
    esnkb = fv(ST0 + 192, 32, "esnk")
    snkb = fv(ST0 + 224, 32, "snk")
    lnst = fv(ST0, 128, "lnst")
    dgp = fv(DOG0, 224, "dgp")
    a1c = fv(DRV0, 32, "a1c")
    nb1 = fv(DRV0 + 32, 32, "nb1")
    a2c = fv(DRV0 + 64, 32, "a2c")
    nb2 = fv(DRV0 + 96, 32, "nb2")
    nega1 = fv(DRV0 + 128, 32, "nega1")
    negc = fv(DRV0 + 160, 32, "negc")
    flagv = fv(FLG0, 8, "flag")
    idb32 = fv(ID32, 8, "idb32")

    kb = KB(nc)
    OP = kb.op
    sems = {}
    ctx = contextlib.ExitStack()
    psS = [ctx.enter_context(nc.psum_tensor(f"psS{i}", [128, 1024], F32)) for i in range(2)]
    psPT = [ctx.enter_context(nc.psum_tensor(f"psPT{i}", [128, 1024], BF16)) for i in range(2)]
    psO = [ctx.enter_context(nc.psum_tensor(f"psO{i}", [128, 512], F32)) for i in range(2)]
    for s in SEMS:
        sems[s] = ctx.enter_context(nc.semaphore(s))

    def psap(t, off, dims, row0=0, nrows=128):
        pitch = 1024 if t in psS or t in psPT else 512
        return bass.AP(t, row0 * pitch + off, [[pitch, nrows]] + dims)

    if dbg:
        hfA = F32_N // 2
        hfB = BF_N // 2
        t_zA = OP("V", lambda v: v.memset(bass.AP(af, 0, [[F32_N, 128], [1, hfA]]),
                                          0.0), tick=True)
        t_zA2 = OP("V", lambda v: v.memset(
            bass.AP(af, hfA, [[F32_N, 128], [1, F32_N - hfA]]), 0.0),
            deps=[t_zA], tick=True)
        t_zB = OP("V", lambda v: v.memset(bass.AP(abf, 0, [[BF_N, 128], [1, hfB]]),
                                          0.0), deps=[t_zA2], tick=True)
        t_zB2 = OP("V", lambda v: v.memset(
            bass.AP(abf, hfB, [[BF_N, 128], [1, BF_N - hfB]]), 0.0),
            deps=[t_zB], tick=True)
        _zdep = [t_zB2]
    else:
        _zdep = []
    gregs = {}

    def f_regs(g):
        gregs["pos"] = g.alloc_register("fillpos")
        gregs["zero"] = g.alloc_register("fillzero")
        g.reg_mov(gregs["pos"], POS)
        return g.reg_mov(gregs["zero"], 0.0)
    xpre = [OP("G", lambda g, tt=tt: g.dma_start(
        out=xtm[tt][:, :],
        in_=dsl(x_in, tt * 128, tt * 128 + 128, 0, DM, DM)),
        deps=list(_zdep), dma="dG") for tt in range(2)]
    OP("G", f_regs, deps=_zdep)

    t_id0 = OP("G", lambda g: g.memset(idb[:, :], 1.0), deps=_zdep, tick=True)
    t_id = OP("G", lambda g: g.affine_select(idb[:, :], idb[:, :], [[-1, 128]],
                                             ALU.is_equal, gregs["zero"], base=0,
                                             channel_multiplier=1),
              deps=[t_id0], tick=True)

    t_i320 = OP("G", lambda g: g.memset(idb32[0:8, :], 1.0), deps=_zdep, tick=True)
    t_id32 = OP("G", lambda g: g.affine_select(idb32[0:8, :], idb32[0:8, :], [[-1, 8]],
                                               ALU.is_equal, gregs["zero"], base=0,
                                               channel_multiplier=1),
               deps=[t_i320], tick=True)

    t_e80 = OP("G", lambda g: g.memset(e8b[0:8, :], 1.0), deps=_zdep, tick=True)
    t_e81 = OP("G", lambda g: g.affine_select(e8b[0:8, :], e8b[0:8, :], [[1, 128]],
                                              ALU.is_ge, gregs["zero"], base=0,
                                              channel_multiplier=-16),
               deps=[t_e80], tick=True)
    t_e8 = OP("G", lambda g: g.affine_select(e8b[0:8, :], e8b[0:8, :], [[-1, 128]],
                                             ALU.is_ge, gregs["zero"], base=15,
                                             channel_multiplier=16),
              deps=[t_e81], tick=True)
    t_ones = OP("G", lambda g: g.memset(onesb[0:1, :], 1.0), deps=_zdep, tick=True)
    t_gd = OP("G", lambda g: g.iota(gdist[:, :], [[-1, 640]], base=512,
                                    channel_multiplier=1,
                                    allow_small_or_imprecise_dtypes=True),
              deps=_zdep, tick=True)
    t_z0 = OP("V", lambda v: v.memset(fap(ST0, [[1, 680]]), 0.0),
              deps=_zdep, tick=True)
    t_dog = OP("Y", lambda y: y.dma_start(out=dgp[:, :], in_=dog_in.ap()),
               deps=[t_z0], dma="dY")
    t_snk = OP("Y", lambda y: y.dma_start(out=snkb[:, :], in_=snk_in.ap()), dma="dY")
    t_cid = OP("Y", lambda y: y.dma_start(out=flagv[0:1, 0:4], in_=cid_in.ap()), dma="dY")

    t_mt0 = OP("G", lambda g: g.memset(gscr[0:1, :], NEG), deps=_zdep, tick=True)
    t_mt0b = OP("G", lambda g: g.memset(gscr2[0:1, :], NEG), deps=[t_mt0], tick=True)
    t_mt1 = OP("G", lambda g: g.affine_select(gscr[0:1, :], gscr[0:1, :], [[-1, 640]],
                                              ALU.is_ge, gregs["zero"], base=511,
                                              channel_multiplier=0),
               deps=[t_mt0], tick=True)
    t_flag = OP("V", lambda v: v.tensor_single_scalar(flagv[0:1, 4:5], flagv[0:1, 0:1],
                                                      0.0, ALU.is_equal),
                deps=[t_cid], tick=True)
    t_mt2 = OP("V", lambda v: v.tensor_scalar_mul(gscr[0:1, :], gscr[0:1, :],
                                                  flagv[0:1, 4:5]),
               deps=[t_mt1, t_flag], tick=True)
    t_mra = OP("V", lambda v: (v.drain(), v.tensor_copy(mrowb[0:1, 0:640],
                                                         gscr[0:1, :]))[-1],
               deps=[t_mt2], tick=True)
    t_mrb = OP("G", lambda g: g.memset(mrowb[0:1, 640:1152], 0.0), deps=_zdep, tick=True)
    t_mrow = t_mra

    t_esnk = OP("S", lambda s: s.activation(esnkb[:, :], snkb[:, :], ACT.Exp),
                deps=[t_snk], tick=True)

    def drv(ps, pm, acoef_t, nb_t, dep0):
        t1 = OP("S", lambda s: s.activation(gscr2[:, 0:32], dgp[:, ps:ps + 32],
                                            ACT.Exp), deps=[t_dog, dep0], tick=True)
        t2 = OP("V", lambda v: v.tensor_scalar_add(gscr2[:, 0:32], gscr2[:, 0:32], 1.0),
                deps=[t1], tick=True)
        t3 = OP("S", lambda s: s.activation(gscr2[:, 0:32], gscr2[:, 0:32], ACT.Ln),
                deps=[t2], tick=True)
        t4 = OP("V", lambda v: v.tensor_scalar_mul(gscr2[:, 0:32], gscr2[:, 0:32], SQ2),
                deps=[t3], tick=True)
        t5 = OP("V", lambda v: (v.drain(), v.reciprocal(acoef_t[:, :], gscr2[:, 0:32]),
                                v.drain())[-1], deps=[t4], tick=True)
        return OP("V", lambda v: v.scalar_tensor_tensor(
            out=nb_t[:, :], in0=dgp[:, pm:pm + 32], scalar=-1.0,
            in1=acoef_t[:, :], op0=ALU.mult, op1=ALU.mult), deps=[t5], tick=True)

    td1 = drv(32, 64, a1c, nb1, t_mt0b)
    td2 = drv(128, 160, a2c, nb2, td1)
    t_na1 = OP("V", lambda v: v.tensor_scalar_mul(nega1[:, :], dgp[:, 0:32], -1.0),
               deps=[t_dog], tick=True)
    t_nc = OP("V", lambda v: v.tensor_scalar_mul(negc[:, :], dgp[:, 192:224], -1.0),
              deps=[t_dog], tick=True)

    def emit_bias_head(h, dst, prev_tf, extra_deps=()):
        base_deps = [td1, td2, t_na1, t_nc, t_gd, t_mra] + list(extra_deps)
        ta = OP("S", lambda s, h=h: s.activation(
            gscr[:, :], gdist[:, :], ACT.Square,
            bias=nb1[:, h:h + 1], scale=a1c[:, h:h + 1]),
            deps=base_deps + [prev_tf], tick=True)
        tb = OP("S", lambda s: s.activation(gscr2[:, :], gscr[:, :], ACT.Exp,
                                            scale=-1.0), deps=[ta], tick=True)
        tc = OP("V", lambda v, h=h: v.tensor_scalar(
            gscr[:, :], gscr2[:, :], nega1[:, h:h + 1], negc[:, h:h + 1],
            ALU.mult, ALU.add), deps=[tb], tick=True)
        td_ = OP("S", lambda s, h=h, dst=dst: s.activation(
            dst[:, :], gdist[:, :], ACT.Square,
            bias=nb2[:, h:h + 1], scale=a2c[:, h:h + 1]),
            deps=base_deps + [prev_tf], tick=True)
        te = OP("S", lambda s, dst=dst: s.activation(dst[:, :], dst[:, :],
                                                     ACT.Exp,
                                                     scale=-1.0), deps=[td_], tick=True)
        tf = OP("V", lambda v, dst=dst, h=h: v.scalar_tensor_tensor(
            out=dst[:, :], in0=dst[:, :], scalar=dgp[:, 96 + h:97 + h],
            in1=gscr[:, :], op0=ALU.mult, op1=ALU.add), deps=[te, tc], tick=True)
        tg1 = OP("G", lambda g, dst=dst: g.affine_select(
            dst[:, :], dst[:, :], [[1, 640]], ALU.is_ge, gregs["pos"],
            base=-1, channel_multiplier=-1), deps=[tf], tick=True)
        tg2 = OP("G", lambda g, dst=dst: g.affine_select(
            dst[:, :], dst[:, :], [[-1, 640]], ALU.is_ge, gregs["pos"],
            base=512, channel_multiplier=1), deps=[tg1], tick=True)
        return tg2, tf

    def build_bias_half(half, extra_deps=()):
        toks = []
        prev_tf = None
        for hh in range(16):
            dst = bneg[hh] if half == 0 else bneg2[hh]
            tg2, prev_tf = emit_bias_head(16 * half + hh, dst, prev_tf, extra_deps)
            toks.append(tg2)
        return toks, prev_tf

    kT_done = []
    vT_done = []
    q_done = []

    def phase_a():
        wsb_free = [None, None]
        wsb4_free = [None, None, None, None]
        qst_free = [None, None]
        xtm_free = [None, None]
        psS_free = [None, None]
        psPT_free = [None, None]
        psO_free = [None, None]
        frees = {}
        for th in range(2):
            t0 = th * 768
            ev_last = None
            for tt in range(6):
                b = tt % 2
                if th == 0 and tt < 2:
                    tok_ld = xpre[tt]
                else:
                    tok_ld = OP("G", lambda g, tt=tt, b=b, t0=t0: g.dma_start(
                        out=xtm[b][:, :],
                        in_=dsl(x_in, t0 + tt * 128, t0 + tt * 128 + 128, 0, DM, DM)),
                        deps=[xtm_free[b]], dma="dG")
                tp_toks = [None, None]
                for i in range(NCH):
                    pb = i % 2
                    tok_tp = OP("T", lambda t, i=i, b=b, pb=pb: t.transpose(
                        psap(psPT[pb], (i // 2) * 128, [[1, 128]]),
                        xtm[b][:, i * 128:(i + 1) * 128], idb[:, :]),
                        deps=[tok_ld, t_id, psPT_free[pb]], tick=(i >= 14))
                    if i >= 14:
                        tp_toks[pb] = tok_tp
                for pb in range(2):
                    tok_ev = OP("V", lambda v, pb=pb, tt=tt: v.tensor_copy(
                        bap(REG + pb * 8 * 768 + tt * 128, [[768, 8], [1, 128]]),
                        psap(psPT[pb], 0, [[128, 8], [1, 128]])),
                        deps=[tp_toks[pb]], tick=True)
                    psPT_free[pb] = tok_ev
                    ev_last = tok_ev
                xtm_free[b] = ev_last
            x_rdy = ev_last

            for o in range(NCH):
                b = o % 4
                pp = o % 2
                tok_w = OP("G", lambda g, o=o, b=b: g.dma_start(
                    out=bap(REG + 16384 + b * 2048, [[128, 16], [1, 128]]),
                    in_=bass.AP(wk_in, o * 128, [[DM, 128], [DM * 128, 16], [1, 128]])),
                    deps=[wsb4_free[b], wsb4_free[(b + 1) % 4],
                          wsb_free[b // 2]], dma="dG")
                last_mm = None
                for (c0, c1, bk) in ((0, 512, 0), (512, 768, 512)):
                    for i in range(NCH):
                        pi = (i % 2) * 8 + i // 2
                        last_mm = OP("T", lambda t, i=i, pi=pi, c0=c0, c1=c1, bk=bk, b=b, pp=pp: t.matmul(
                            psap(psS[pp], bk, [[1, c1 - c0]]),
                            wsb4[b][:, i * 128:(i + 1) * 128],
                            xfm[pi][:, c0:c1],
                            start=(i == 0), stop=(i == NCH - 1)),
                            deps=[tok_w, x_rdy, psS_free[pp]],
                            tick=(i == NCH - 1))
                wsb4_free[b] = last_mm
                tok_ev = OP("S", lambda s, o=o, th=th, pp=pp: s.activation(
                    kT[o][:, th * 768:(th + 1) * 768],
                    psap(psS[pp], 0, [[1, 768]]), ACT.Copy),
                    deps=[last_mm], tick=True)
                psS_free[pp] = tok_ev
                if th == 1:
                    kT_done.append(tok_ev)

            qt0 = max(512, t0)
            qw = t0 + 768 - qt0
            for o in range(NCH):
                b = o % 4
                tok_w = OP("G", lambda g, o=o, b=b: g.dma_start(
                    out=bap(REG + 16384 + b * 2048, [[128, 16], [1, 128]]),
                    in_=bass.AP(wq_in, o * 128, [[DM, 128], [DM * 128, 16], [1, 128]])),
                    deps=[wsb4_free[b], wsb4_free[(b + 1) % 4]], dma="dG")
                last_mm = None
                rngs = ((0, 256, 0),) if qw == 256 else ((0, 512, 0), (512, 768, 512))
                pp = o % 2
                for (c0, c1, bk) in rngs:
                    for i in range(NCH):
                        pi = (i % 2) * 8 + i // 2
                        xc0 = qt0 - t0 + c0
                        last_mm = OP("T", lambda t, i=i, pi=pi, c0=c0, c1=c1, bk=bk, b=b, xc0=xc0, pp=pp: t.matmul(
                            psap(psS[pp], bk, [[1, c1 - c0]]),
                            wsb4[b][:, i * 128:(i + 1) * 128],
                            xfm[pi][:, xc0:xc0 + (c1 - c0)],
                            start=(i == 0), stop=(i == NCH - 1)),
                            deps=[tok_w, x_rdy, psS_free[pp]],
                            tick=(i == NCH - 1))
                wsb4_free[b] = last_mm
                qb_ = o % 2
                tok_ev = OP("S", lambda s, qw=qw, qb_=qb_, pp=pp: s.activation(
                    qst[qb_][:, 0:qw], psap(psS[pp], 0, [[1, qw]]),
                    ACT.Copy, scale=SCALE), deps=[last_mm, qst_free[qb_]], tick=True)
                psS_free[pp] = tok_ev
                tok_st = OP("Y", lambda y, o=o, qt0=qt0, qw=qw, qb_=qb_: y.dma_start(
                    out=dsl(q_dram, o * 128, o * 128 + 128, qt0 - 512, qt0 - 512 + qw, TOWN),
                    in_=qst[qb_][:, 0:qw]), deps=[tok_ev], dma="dY")
                qst_free[qb_] = tok_st
                if th == 1:
                    q_done.append(tok_st)

            for og in range(8):
                b = og % 2
                tok_w = OP("G", lambda g, og=og, b=b: g.dma_start(
                    out=bap(REG + 16384 + b * 4096, [[256, 16], [1, 256]]),
                    in_=bass.AP(wv_in, og * 256, [[DM, 128], [DM * 128, 16], [1, 256]])),
                    deps=[wsb_free[b]] + list(wsb4_free), dma="dG")
                last_mm = None
                for tt in range(6):
                    pb = tt % 2
                    for i in range(NCH):
                        pi = (i % 2) * 8 + i // 2
                        last_mm = OP("T", lambda t, i=i, pi=pi, tt=tt, b=b, pb=pb: t.matmul(
                            psap(psO[pb], 0, [[1, 256]]),
                            xfm[pi][:, tt * 128:(tt + 1) * 128],
                            wsb[b][:, i * 256:(i + 1) * 256],
                            start=(i == 0), stop=(i == NCH - 1)),
                            deps=[tok_w, x_rdy, psO_free[pb]],
                            tick=(i == NCH - 1))
                    jj = th * 6 + tt
                    tok_ev = OP("V", lambda v, jj=jj, og=og, pb=pb: v.tensor_copy(
                        vT[jj][:, og * 256:(og + 1) * 256],
                        psap(psO[pb], 0, [[1, 256]])),
                        deps=[last_mm], tick=True)
                    psO_free[pb] = tok_ev
                    if og == 7:
                        vT_done.append(tok_ev)
                wsb_free[b] = last_mm
        frees["psS"] = psS_free
        frees["psPT"] = psPT_free
        frees["psO"] = psO_free
        return frees

    fA = phase_a()
    bias0, bias0_tf = build_bias_half(0)

    attn_done = []

    def attention():
        qb_free = [None, None]
        psS_free = list(fA["psS"])
        psPT_free = list(fA["psPT"])
        psO_free = list(fA["psO"])
        pb_free = [None] * 8
        ptsb_free = [None, None]
        b1 = {"hh": 0, "prev": bias0_tf, "toks": []}
        stt_last = [None] * 16

        def emit_b1_heads(k):
            for _ in range(k):
                if b1["hh"] >= 16:
                    return
                hh = b1["hh"]
                tg2, b1["prev"] = emit_bias_head(
                    16 + hh, bneg[hh], b1["prev"], extra_deps=[stt_last[hh]])
                b1["toks"].append(tg2)
                b1["hh"] = hh + 1
        btoks = bias0
        for half in range(2):
            if half == 1:
                emit_b1_heads(16)
                btoks = b1["toks"]
            for n in range(2):
                for qc in range(4):
                    w0 = n * 512 + qc * 128
                    qbi = (n * 4 + qc) % 2
                    tok_q = OP("Y", lambda y, w0=w0, qbi=qbi: y.dma_start(
                        out=bap(QB0 + qbi * 2048, [[128, 16], [1, 128]]),
                        in_=bass.AP(q_dram, w0, [[TOWN, 128], [TOWN * 128, 16], [1, 128]])),
                        deps=q_done + [qb_free[qbi], vT_done[-1]], dma="dY")
                    last_pv = None
                    for hh in range(16):
                        h = 16 * half + hh
                        sb = hh % 2
                        kch = h // 2
                        krs = (h % 2) * 64
                        tp = (64, 0) if krs else None
                        mm_deps = [tok_q, psS_free[sb], btoks[hh], t_ones, t_mrb] + \
                            (kT_done if (half, n, qc) == (0, 0, 0) else [])

                        def s_mm(t, kch=kch, krs=krs, qbi=qbi, w0=w0, sb=sb, tp=tp, n=n, qc=qc):
                            lhs = bap(QB0 + qbi * 2048 + kch * 128, [[1, 128]],
                                      row0=krs, nrows=64)
                            t.matmul(psap(psS[sb], 0, [[1, 512]]), lhs,
                                     kT[kch][krs:krs + 64, w0:w0 + 512],
                                     start=True, stop=(n == 1), tile_position=tp)
                            if n == 0:
                                t.matmul(psap(psS[sb], 0, [[1, 512]]),
                                         onesb[0:1, :],
                                         mrowb[0:1, qc * 128:qc * 128 + 512],
                                         start=False, stop=True)
                            return t.matmul(psap(psS[sb], 512, [[1, 128]]), lhs,
                                            kT[kch][krs:krs + 64, w0 + 512:w0 + 640],
                                            start=True, stop=True, tile_position=tp)
                        tok_s = OP("T", s_mm, deps=mm_deps, tick=True)
                        tok_t = OP("V", lambda v, sb=sb, hh=hh: v.scalar_tensor_tensor(
                            out=psap(psS[sb], 0, [[1, 640]]),
                            in0=psap(psS[sb], 0, [[1, 640]]),
                            scalar=-1.0, in1=bneg[hh][:, :],
                            op0=ALU.mult, op1=ALU.add), deps=[tok_s], tick=True)
                        if half == 0 and (n, qc) == (1, 3):
                            stt_last[hh] = tok_t
                        k = hh % 8
                        tok_e = OP("S", lambda s, sb=sb, h=h, k=k: s.activation(
                            pbuf[k][:, :], psap(psS[sb], 0, [[1, 640]]),
                            ACT.Exp, scale=-1.0,
                            accum_out=dsum[:, h:h + 1]),
                            deps=[tok_t, pb_free[k]], tick=True)
                        psS_free[sb] = tok_e

                        if hh % 8 == 7:
                            h0 = h - 7
                            gs = slice(h0, h0 + 8)
                            tok_c = OP("V", lambda v, gs=gs: v.tensor_add(
                                denv[:, gs], dsum[:, gs], esnkb[:, gs]),
                                deps=[tok_e, t_esnk], tick=True)
                            tok_d = OP("V", lambda v, gs=gs: (v.drain(), v.reciprocal(
                                rdv[:, gs], denv[:, gs]), v.drain())[-1],
                                deps=[tok_c], tick=True)
                            for hh2 in range(hh - 7, hh + 1):
                                h2 = 16 * half + hh2
                                k2 = hh2 % 8
                                tok_sc = OP("V", lambda v, k2=k2, h2=h2: v.tensor_scalar_mul(
                                    pbuf[k2][:, :], pbuf[k2][:, :], rdv[:, h2:h2 + 1]),
                                    deps=[tok_d, pb_free[k2]], tick=True)
                                ptb = hh2 % 2
                                tok_tp = None
                                for i5 in range(5):
                                    tok_tp = OP("T", lambda t, k2=k2, i5=i5, ptb=ptb: t.transpose(
                                        psap(psPT[ptb], i5 * 128, [[1, 128]]),
                                        pbuf[k2][:, i5 * 128:(i5 + 1) * 128],
                                        idb[:, :]),
                                        deps=[tok_sc, psPT_free[ptb], t_id],
                                        tick=(i5 == 4))
                                pb_free[k2] = tok_tp
                                tok_pte = OP("V", lambda v, ptb=ptb: v.tensor_copy(
                                    ptsb[ptb][:, :], psap(psPT[ptb], 0, [[1, 640]])),
                                    deps=[tok_tp, ptsb_free[ptb]], tick=True)
                                psPT_free[ptb] = tok_pte
                                pair = h2 // 2
                                ob = (pair % 8) // 4
                                slot = pair % 4
                                orow = (h2 % 2) * 64
                                pv_deps = [tok_pte, psO_free[ob]] + \
                                    (vT_done if (half, n, qc) == (0, 0, 0) else [])

                                def pv(t, h2=h2, n=n, qc=qc, slot=slot, orow=orow,
                                       ob=ob, ptb=ptb):
                                    ins = None
                                    for i5 in range(5):
                                        vch = n * 4 + qc + i5
                                        ins = t.matmul(
                                            psap(psO[ob], slot * 128, [[1, 128]],
                                                 row0=orow, nrows=64),
                                            vT[vch][:, h2 * 64:(h2 + 1) * 64],
                                            ptsb[ptb][:, i5 * 128:(i5 + 1) * 128],
                                            start=(i5 == 0), stop=(i5 == 4),
                                            tile_position=((0, 64) if orow else None))
                                    return ins
                                tok_pv = OP("T", pv, deps=pv_deps, tick=True)
                                ptsb_free[ptb] = tok_pv
                                last_pv = tok_pv
                                if h2 % 8 == 7:
                                    pg = h2 // 8

                                    def oev(se, pg=pg, n=n, qc=qc, ob=ob):
                                        return se.activation(
                                            bap(AFM0 + n * 8192 + pg * 2048 + qc * 128,
                                                [[512, 4], [1, 128]]),
                                            psap(psO[ob], 0, [[128, 4], [1, 128]]),
                                            ACT.Copy)
                                    tok_oe = OP("S", oev, deps=[tok_pv], tick=True)
                                    psO_free[ob] = tok_oe
                                    if half == 1:
                                        attn_done.append(tok_oe)
                    qb_free[qbi] = last_pv
                    if half == 0 and (n, qc) == (1, 3):
                        emit_b1_heads(16)
        return {"psS": psS_free, "psPT": psPT_free, "psO": psO_free,
                "ptsb": ptsb_free}

    fAT = attention()

    if stop_after == "attn":
        toks = attn_done
        d_pb = nc.dram_tensor("d_pb", [128, 5120], BF16, kind="ExternalOutput")
        d_pt = nc.dram_tensor("d_pt", [128, 1280], BF16, kind="ExternalOutput")
        t1 = OP("Y", lambda y: y.dma_start(out=d_pb.ap(), in_=bap(REG, [[1, 5120]])),
                deps=toks + [kb.cnt and None], dma="dY")
        t2 = OP("Y", lambda y: y.dma_start(out=d_pt.ap(), in_=bap(REG + 5120, [[1, 1280]])),
                deps=toks, dma="dY")
        if dbg:
            for nm_, tgt, srcap in [
                ("d_bneg", d_bneg, fap(BN0, [[1, 10240]])),
                ("d_afm", d_afm, bap(AFM0, [[1, 16384]])),
                ("d_stat", d_stat, fap(ST0, [[1, 688]])),
            ]:
                OP("Y", lambda y, tgt=tgt, srcap=srcap: y.dma_start(
                    out=tgt.ap(), in_=srcap), deps=toks, dma="dY")
        _afins = kb.dma_total("dY")
        def f_afin(y, fins=_afins):
            ins = None
            for s_, v_ in fins:
                ins = y.wait_ge(sems[s_], v_)
            return ins
        OP("Y", f_afin)
        with nc.Block() as block:
            kb.emit(block, sems)
        ctx.close()
        return nc

    y1_done = []
    yfm_done = []

    def ln_block(r1_deps, sq_free, y_free, out_tile, rb, so):
        tok_sum = OP("V", lambda v: v.reduce_sum(lnst[:, so:so + 1], rb[:, :], axis=AX),
                     deps=r1_deps, tick=True)
        tok_sq = OP("S", lambda s: s.activation(sqb[:, :], rb[:, :], ACT.Square,
                                                accum_out=lnst[:, so + 1:so + 2]),
                    deps=r1_deps + [sq_free], tick=True)
        tok_mean = OP("V", lambda v: v.tensor_scalar_mul(lnst[:, so + 2:so + 3],
                                                         lnst[:, so:so + 1],
                                                         1.0 / DM),
                      deps=[tok_sum], tick=True)
        tok_msq = OP("V", lambda v: v.tensor_mul(lnst[:, so + 3:so + 4],
                                                 lnst[:, so + 2:so + 3],
                                                 lnst[:, so + 2:so + 3]),
                     deps=[tok_mean], tick=True)
        tok_var = OP("V", lambda v: v.scalar_tensor_tensor(
            out=lnst[:, so + 4:so + 5], in0=lnst[:, so + 1:so + 2], scalar=1.0 / DM,
            in1=lnst[:, so + 3:so + 4], op0=ALU.mult, op1=ALU.subtract),
            deps=[tok_sq, tok_msq], tick=True)
        tok_vare = OP("V", lambda v: v.tensor_scalar_add(lnst[:, so + 4:so + 5],
                                                         lnst[:, so + 4:so + 5],
                                                         EPS), deps=[tok_var], tick=True)
        tok_ln = OP("S", lambda s: s.activation(lnst[:, so + 5:so + 6],
                                                lnst[:, so + 4:so + 5], ACT.Ln),
                    deps=[tok_vare], tick=True)
        tok_rs = OP("S", lambda s: s.activation(lnst[:, so + 6:so + 7],
                                                lnst[:, so + 5:so + 6], ACT.Exp,
                                                scale=-0.5), deps=[tok_ln], tick=True)
        tok_nmr = OP("V", lambda v: v.tensor_scalar(lnst[:, so + 7:so + 8],
                                                    lnst[:, so + 2:so + 3],
                                                    lnst[:, so + 6:so + 7], -1.0,
                                                    ALU.mult, ALU.mult),
                     deps=[tok_rs], tick=True)
        tok_ap = OP("S", lambda s: s.activation(out_tile[:, :], rb[:, :], ACT.Identity,
                                                bias=lnst[:, so + 7:so + 8],
                                                scale=lnst[:, so + 6:so + 7]),
                    deps=[tok_nmr] + list(y_free), tick=True)
        return tok_ap, tok_sq

    def wo_r1_ln1():
        wsb_free = [None, None]
        psO_free = list(fAT["psO"])
        wo_ev = {}
        for o in range(NCH):
            b = o % 2
            tok_w = OP("G", lambda g, o=o, b=b: g.dma_start(
                out=bap(REG + 6400 + b * 2048, [[128, 16], [1, 128]]),
                in_=bass.AP(wo_in, o * 128, [[DM, 128], [DM * 128, 16], [1, 128]])),
                deps=[wsb_free[b]] + (attn_done if o < 2 else []), dma="dG")
            last_mm = None
            for n in range(2):
                pb = (o * 2 + n) % 2
                for pair in range(16):
                    last_mm = OP("T", lambda t, pair=pair, n=n, b=b, pb=pb: t.matmul(
                        psap(psO[pb], 0, [[1, 512]]),
                        wsb2[b][:, pair * 128:(pair + 1) * 128],
                        bap(AFM0 + n * 8192 + pair * 512, [[1, 512]]),
                        start=(pair == 0), stop=(pair == 15)),
                        deps=[tok_w, psO_free[pb]] + attn_done,
                        tick=(pair == 15))
                tok_ev = OP("V", lambda v, o=o, n=n, pb=pb: v.tensor_copy(
                    wofm[0:128, o * 1024 + n * 512:o * 1024 + n * 512 + 512],
                    psap(psO[pb], 0, [[1, 512]])),
                    deps=[last_mm], tick=True)
                psO_free[pb] = tok_ev
                wo_ev[(o, n)] = tok_ev
            wsb_free[b] = last_mm

        xb_free = [attn_done[-1], attn_done[-1]]
        psPT_free = list(fAT["psPT"])
        r1_free = [None, None]
        y1_free = []
        sq_free = fAT["psS"][0]
        for n in range(2):
            for ts_ in range(4):
                gt = n * 512 + ts_ * 128
                bx = (n * 4 + ts_) % 2
                tok_x = OP("Y", lambda y, gt=gt, bx=bx: y.dma_start(
                    out=xb[bx][:, :],
                    in_=dsl(x_in, 512 + gt, 512 + gt + 128, 0, DM, DM)),
                    deps=[xb_free[bx]], dma="dY")
                tpt = [None, None]
                for o in range(NCH):
                    tok_tp = OP("T", lambda t, o=o, n=n, ts_=ts_: t.transpose(
                        psap(psPT[o // 8], (o % 8) * 128, [[1, 128]]),
                        wofm[0:128, o * 1024 + n * 512 + ts_ * 128:
                             o * 1024 + n * 512 + ts_ * 128 + 128],
                        idb[:, :]),
                        deps=[wo_ev[(o, n)], t_id, psPT_free[o // 8]],
                        tick=(o % 8 == 7))
                    if o % 8 == 7:
                        tpt[o // 8] = tok_tp
                rb = r1b if (n * 4 + ts_) % 2 == 0 else r1c
                so = 16 * ((n * 4 + ts_) % 2)
                tok_r1a = OP("V", lambda v, bx=bx, rb=rb: v.tensor_add(
                    rb[:, 0:1024], psap(psPT[0], 0, [[1, 1024]]),
                    xb[bx][:, 0:1024]), deps=[tpt[0], tok_x, r1_free[(n * 4 + ts_) % 2]], tick=True)
                tok_r1b = OP("V", lambda v, bx=bx, rb=rb: v.tensor_add(
                    rb[:, 1024:2048], psap(psPT[1], 0, [[1, 1024]]),
                    xb[bx][:, 1024:2048]), deps=[tpt[1], tok_x], tick=True)
                psPT_free[0] = tok_r1a
                psPT_free[1] = tok_r1b
                xb_free[bx] = tok_r1b
                tok_y1, tok_sq = ln_block([tok_r1a, tok_r1b], sq_free, y1_free,
                                          y1b, rb, so)
                sq_free = tok_sq
                r1_free[(n * 4 + ts_) % 2] = tok_y1
                tok_st = OP("Y", lambda y, gt=gt: y.dma_start(
                    out=dsl(y1_dram, gt, gt + 128, 0, DM, DM), in_=y1b[:, :]),
                    deps=[tok_y1], dma="dY")
                y1_done.append(tok_st)
                tok_ybf = OP("V", lambda v: v.tensor_copy(sqb[:, :], y1b[:, :]),
                             deps=[tok_y1, tok_sq], tick=True)
                sq_free = tok_ybf
                tpt2 = [None, None]
                for i in range(NCH):
                    tok_tp = OP("T", lambda t, i=i: t.transpose(
                        psap(psPT[i // 8], (i % 8) * 128, [[1, 128]]),
                        sqb[:, i * 128:(i + 1) * 128], idb[:, :]),
                        deps=[tok_ybf, t_id, psPT_free[i // 8]], tick=(i % 8 == 7))
                    if i % 8 == 7:
                        tpt2[i // 8] = tok_tp
                for pb in range(2):
                    tok_ev = OP("V", lambda v, pb=pb, n=n, ts_=ts_: v.tensor_copy(
                        bap(YFM0 + pb * 8 * 1024 + n * 512 + ts_ * 128,
                            [[1024, 8], [1, 128]]),
                        psap(psPT[pb], 0, [[128, 8], [1, 128]])),
                        deps=[tpt2[pb]], tick=True)
                    psPT_free[pb] = tok_ev
                    yfm_done.append(tok_ev)
                y1_free = [tok_st, tok_ybf]
        return {"psO": psO_free, "psPT": psPT_free, "sq": sq_free,
                "xb": xb_free, "y1": y1_free, "r1": r1_free[1] or r1_free[0],
                "psS": fAT["psS"], "wsb": wsb_free}

    fWO = wo_r1_ln1()

    if stop_after == "wo":
        toks = [fWO["r1"], y1_done[-1], yfm_done[-1]]
        for nm_, tgt, srcap in [
            ("d_afm", d_afm, bap(AFM0, [[1, 16384]])),
            ("d_wofm", d_wofm, bap(0, [[1, 16384]])),
            ("d_yfm", d_yfm, bap(YFM0, [[1, 16384]])),
        ]:
            OP("Y", lambda y, tgt=tgt, srcap=srcap: y.dma_start(
                out=tgt.ap(), in_=srcap), deps=toks, dma="dY")
        _wfins = kb.dma_total("dY")
        def f_wfin(y, fins=_wfins):
            ins = None
            for s_, v_ in fins:
                ins = y.wait_ge(sems[s_], v_)
            return ins
        OP("Y", f_wfin)
        with nc.Block() as block:
            kb.emit(block, sems)
        ctx.close()
        return nc

    def ffn():
        guard = [attn_done[-1]]
        tok_at = OP("G", lambda g: g.dma_start(
            out=bap(REG, [[128, 16], [1, 128]]),
            in_=bass.AP(at_in, 0, [[128, 128], [128 * 128, 16], [1, 128]])),
            deps=guard, dma="dG")
        tok_bt = OP("G", lambda g: g.dma_start(out=btb[:, :], in_=bt_in.ap()),
                    deps=guard, dma="dG")
        tok_rw = OP("G", lambda g: g.dma_start(
            out=bap(RW0, [[8, 16], [1, 8]]),
            in_=bass.AP(rwt_in, 0, [[8, 128], [1024, 16], [1, 8]])),
            deps=guard, dma="dG")

        psO_free = list(fWO["psO"])
        psS_free = list(fWO["psS"])
        psPT_free = list(fWO["psPT"])
        scax_tok = [fWO["r1"], fWO["r1"]]
        for n in range(2):
            last_mm = None
            for i in range(NCH):
                last_mm = OP("T", lambda t, i=i, n=n: t.matmul(
                    psap(psO[0], 0, [[1, 512]], nrows=8),
                    rwb[:, i * 8:(i + 1) * 8],
                    yfm[i][:, n * 512:n * 512 + 512],
                    start=(i == 0), stop=(i == NCH - 1)),
                    deps=[tok_rw, psO_free[0]] + yfm_done, tick=(i == NCH - 1))
            tok_lge = OP("V", lambda v: v.tensor_copy(
                lgfm[0:8, :], psap(psO[0], 0, [[1, 512]], nrows=8)),
                deps=[last_mm], tick=True)
            psO_free[0] = tok_lge
            last_ax = None
            for i in range(NCH):
                last_ax = OP("T", lambda t, i=i, n=n: t.matmul(
                    psap(psO[1], 0, [[1, 512]]),
                    atb[:, i * 128:(i + 1) * 128],
                    yfm[i][:, n * 512:n * 512 + 512],
                    start=(i == 0), stop=(i == NCH - 1)),
                    deps=[tok_at, psO_free[1]] + yfm_done, tick=(i == NCH - 1))
            wfm_toks = []
            for q in range(4):
                tok_tp = OP("T", lambda t, q=q: t.transpose(
                    psap(psS[0], 0, [[1, 8]]),
                    lgfm[0:8, q * 128:(q + 1) * 128], idb32[0:8, :]),
                    deps=[tok_lge, t_id32, psS_free[0]], tick=True)
                tok_lgt = OP("V", lambda v: v.tensor_copy(
                    lgtm[:, 0:8], psap(psS[0], 0, [[1, 8]])),
                    deps=[tok_tp], tick=True)
                psS_free[0] = tok_lgt
                tok_m1 = OP("V", lambda v: v.reduce_max(rst[:, 0:1], lgtm[:, 0:8],
                                                        axis=AX), deps=[tok_lgt], tick=True)
                tok_nm1 = OP("V", lambda v: v.reduce_max(rst[:, 1:2], lgtm[:, 0:8],
                                                         axis=AX, negate=True),
                             deps=[tok_lgt], tick=True)
                tok_ind = OP("V", lambda v: v.tensor_single_scalar(
                    rst[:, 8:16], lgtm[:, 0:8], rst[:, 0:1], ALU.is_ge),
                    deps=[tok_m1], tick=True)
                tok_msk = OP("V", lambda v: v.scalar_tensor_tensor(
                    out=rst[:, 16:24], in0=rst[:, 8:16], scalar=NEG,
                    in1=lgtm[:, 0:8], op0=ALU.mult, op1=ALU.add),
                    deps=[tok_ind], tick=True)
                tok_m2 = OP("V", lambda v: v.reduce_max(rst[:, 2:3], rst[:, 16:24],
                                                        axis=AX), deps=[tok_msk], tick=True)
                tok_ind2 = OP("V", lambda v: v.tensor_single_scalar(
                    rst[:, 8:16], lgtm[:, 0:8], rst[:, 2:3], ALU.is_ge),
                    deps=[tok_m2], tick=True)
                tok_u = OP("S", lambda s: s.activation(
                    rst[:, 24:32], lgtm[:, 0:8], ACT.Exp, bias=rst[:, 1:2]),
                    deps=[tok_nm1, tok_msk], tick=True)
                tok_s2 = OP("S", lambda s: s.activation(
                    rst[:, 3:4], rst[:, 2:3], ACT.Exp, bias=rst[:, 1:2]),
                    deps=[tok_m2], tick=True)
                tok_nd = OP("V", lambda v: v.tensor_scalar_add(rst[:, 4:5], rst[:, 3:4],
                                                               1.0), deps=[tok_s2], tick=True)
                tok_rn = OP("V", lambda v: (v.drain(), v.reciprocal(rst[:, 5:6], rst[:, 4:5]),
                                            v.drain())[-1], deps=[tok_nd], tick=True)
                tok_w1 = OP("V", lambda v: v.tensor_mul(rst[:, 8:16], rst[:, 8:16],
                                                        rst[:, 24:32]),
                            deps=[tok_u, tok_ind2], tick=True)
                tok_wv = OP("V", lambda v: (v.drain(), v.tensor_scalar(
                    wtm[:, 0:8], rst[:, 8:16], rst[:, 5:6], 2.0, ALU.mult,
                    ALU.mult))[-1], deps=[tok_w1, tok_rn], tick=True)
                tok_wt = OP("T", lambda t: t.transpose(
                    psap(psPT[0], 0, [[1, 128]], nrows=8),
                    wtm[:, 0:8], idb[:, :]),
                    deps=[tok_wv, psPT_free[0], t_id], tick=True)
                tok_we = OP("V", lambda v, q=q: v.tensor_copy(
                    wfm[0:8, q * 128:(q + 1) * 128],
                    psap(psPT[0], 0, [[1, 128]], nrows=8)),
                    deps=[tok_wt], tick=True)
                psPT_free[0] = tok_we
                wfm_toks.append(tok_we)
            tok_wr = OP("T", lambda t: t.matmul(
                psap(psS[0], 0, [[1, 512]]),
                e8b[0:8, :], wfm[0:8, :], start=True, stop=True),
                deps=wfm_toks + [t_e8, psS_free[0]], tick=True)
            tok_wre = OP("V", lambda v: v.tensor_copy(
                wrepf[:, :], psap(psS[0], 0, [[1, 512]])),
                deps=[tok_wr], tick=True)
            psS_free[0] = tok_wre
            tok_sc = OP("V", lambda v, n=n: v.tensor_mul(
                scax[n][:, :], psap(psO[1], 0, [[1, 512]]), wrepf[:, :]),
                deps=[last_ax, tok_wre, scax_tok[n]], tick=True)
            psO_free[1] = tok_sc
            scax_tok[n] = tok_sc

        wsb_free = list(fWO["wsb"])
        psO_free2 = [psO_free[0], psO_free[1]]
        ffm_ev = {}
        for o in range(NCH):
            b = o % 2
            tok_w = OP("G", lambda g, o=o, b=b: g.dma_start(
                out=bap(REG + 6400 + b * 2048, [[128, 16], [1, 128]]),
                in_=bass.AP(wt_in, o * 128, [[DM, 128], [DM * 128, 16], [1, 128]])),
                deps=[wsb_free[b]] + (list(fWO["y1"]) + [yfm_done[-1], fWO["sq"]]
                                      if o < 2 else []),
                dma="dG")
            last = None
            for n in range(2):
                pb = (o * 2 + n) % 2
                for i in range(NCH):
                    OP("T", lambda t, i=i, n=n, b=b, pb=pb: t.matmul(
                        psap(psO[pb], 0, [[1, 512]]),
                        wsb2[b][:, i * 128:(i + 1) * 128],
                        yfm[i][:, n * 512:n * 512 + 512],
                        start=(i == 0), stop=False),
                        deps=[tok_w, psO_free2[pb]] + yfm_done, tick=False)
                last = OP("T", lambda t, o=o, n=n, pb=pb: t.matmul(
                    psap(psO[pb], 0, [[1, 512]]),
                    btb[:, o * 128:(o + 1) * 128],
                    scax[n][:, :], start=False, stop=True),
                    deps=[tok_bt, scax_tok[n]], tick=True)
                tok_ev = OP("V", lambda v, o=o, n=n, pb=pb: v.tensor_copy(
                    ffm[0:128, o * 1024 + n * 512:o * 1024 + n * 512 + 512],
                    psap(psO[pb], 0, [[1, 512]])),
                    deps=[last], tick=True)
                psO_free2[pb] = tok_ev
                ffm_ev[(o, n)] = tok_ev
            wsb_free[b] = last

        yb_free = list(fWO["xb"])
        r2_free = [fWO["r1"], fWO["r1"]]
        o_free = [y1_done[-1]] + fWO["y1"]
        sq_free = fWO["sq"]
        out_toks = []
        for n in range(2):
            for ts_ in range(4):
                gt = n * 512 + ts_ * 128
                bx = (n * 4 + ts_) % 2
                tok_yl = OP("Y", lambda y, gt=gt, bx=bx: y.dma_start(
                    out=xb[bx][:, :], in_=dsl(y1_dram, gt, gt + 128, 0, DM, DM)),
                    deps=[yb_free[bx]] + y1_done, dma="dY")
                tpt = [None, None]
                for o in range(NCH):
                    tok_tp = OP("T", lambda t, o=o, n=n, ts_=ts_: t.transpose(
                        psap(psPT[o // 8], (o % 8) * 128, [[1, 128]]),
                        ffm[0:128, o * 1024 + n * 512 + ts_ * 128:
                            o * 1024 + n * 512 + ts_ * 128 + 128],
                        idb[:, :]),
                        deps=[ffm_ev[(o, n)], t_id, psPT_free[o // 8]],
                        tick=(o % 8 == 7))
                    if o % 8 == 7:
                        tpt[o // 8] = tok_tp
                rb = r1b if (n * 4 + ts_) % 2 == 0 else r1c
                so = 16 * ((n * 4 + ts_) % 2)
                tok_r2a = OP("V", lambda v, bx=bx, rb=rb: v.tensor_add(
                    rb[:, 0:1024], psap(psPT[0], 0, [[1, 1024]]),
                    xb[bx][:, 0:1024]), deps=[tpt[0], tok_yl, r2_free[(n * 4 + ts_) % 2]], tick=True)
                tok_r2b = OP("V", lambda v, bx=bx, rb=rb: v.tensor_add(
                    rb[:, 1024:2048], psap(psPT[1], 0, [[1, 1024]]),
                    xb[bx][:, 1024:2048]), deps=[tpt[1], tok_yl], tick=True)
                psPT_free[0] = tok_r2a
                psPT_free[1] = tok_r2b
                yb_free[bx] = tok_r2b
                tok_o, tok_sq = ln_block([tok_r2a, tok_r2b], sq_free, o_free,
                                         y1b, rb, so)
                sq_free = tok_sq
                r2_free[(n * 4 + ts_) % 2] = tok_o
                tok_st = OP("Y", lambda y, gt=gt: y.dma_start(
                    out=dsl(y_out, gt, gt + 128, 0, DM, DM), in_=y1b[:, :]),
                    deps=[tok_o], dma="dY")
                o_free = [tok_st]
                out_toks.append(tok_st)
        _fins = kb.dma_total("dY")
        def f_fin(y, fins=_fins):
            ins = None
            for s_, v_ in fins:
                ins = y.wait_ge(sems[s_], v_)
            return ins
        OP("Y", f_fin, deps=out_toks)

    ffn()

    if dbg:
        for nm_, src, dep in [
            ("d_bneg", [fap(BN0, [[1, 8192]])], None),
            ("d_afm", [bap(AFM0, [[1, 16384]])], None),
            ("d_wofm", [bap(0, [[1, 16384]])], None),
            ("d_yfm", [bap(YFM0, [[1, 16384]])], None),
            ("d_ffm", [bap(FFM0, [[1, 16384]])], None),
            ("d_stat", [fap(ST0, [[1, 688]])], None),
            ("d_scax", [bap(REG + 4224, [[1, 1024]])], None),
        ]:
            tgt = {"d_bneg": d_bneg, "d_afm": d_afm, "d_wofm": d_wofm,
                   "d_yfm": d_yfm, "d_ffm": d_ffm, "d_stat": d_stat,
                   "d_scax": d_scax}[nm_]
            OP("Y", lambda y, tgt=tgt, src=src: y.dma_start(
                out=tgt.ap(), in_=src[0]), deps=[], dma="dY")
        _fins2 = kb.dma_total("dY")
        def f_fin2(y, fins=_fins2):
            ins = None
            for s_, v_ in fins:
                ins = y.wait_ge(sems[s_], v_)
            return ins
        OP("Y", f_fin2)

    with nc.Block() as block:
        kb.emit(block, sems)
    ctx.close()
    return nc


def kernel(L0_attn_x, L0_attn_Wq, L0_attn_Wk, L0_attn_Wv, L0_attn_Wo,
           L0_attn_dog_params, L0_attn_sinks, L0_ffn_weight, L0_ffn_lora_A,
           L0_ffn_lora_B, L0_ffn_router_w):
    x = np.ascontiguousarray(np.asarray(L0_attn_x, dtype=np.float32).reshape(8192, DM))
    wq = np.ascontiguousarray(np.asarray(L0_attn_Wq, dtype=np.float32))
    wk = np.ascontiguousarray(np.asarray(L0_attn_Wk, dtype=np.float32))
    wv = np.ascontiguousarray(np.asarray(L0_attn_Wv, dtype=np.float32))
    wo = np.ascontiguousarray(np.asarray(L0_attn_Wo, dtype=np.float32))
    wt = np.ascontiguousarray(np.asarray(L0_ffn_weight, dtype=np.float32).T)
    at = np.ascontiguousarray(
        np.asarray(L0_ffn_lora_A, dtype=np.float32).reshape(128, DM).T)
    bt = np.ascontiguousarray(
        np.transpose(np.asarray(L0_ffn_lora_B, dtype=np.float32), (0, 2, 1)).reshape(128, DM))
    rwt = np.ascontiguousarray(np.asarray(L0_ffn_router_w, dtype=np.float32).T)
    dog = np.ascontiguousarray(np.broadcast_to(
        np.asarray(L0_attn_dog_params, dtype=np.float32).T.reshape(1, 224), (128, 224)))
    snk = np.ascontiguousarray(np.broadcast_to(
        np.asarray(L0_attn_sinks, dtype=np.float32).reshape(1, 32), (128, 32)))

    nc = bass.Bass("TRN2", target_bir_lowering=False)
    build_program(nc)

    in_maps = []
    for c in range(NCORES):
        lo = c * TOWN - HALO
        if lo < 0:
            xs = np.concatenate([np.zeros((HALO, DM), np.float32),
                                 x[0:c * TOWN + TOWN]], axis=0)
        else:
            xs = x[lo:c * TOWN + TOWN]
        in_maps.append({
            "x": np.ascontiguousarray(xs), "wq": wq, "wk": wk, "wv": wv,
            "wo": wo, "wt": wt, "at": at, "bt": bt, "rwt": rwt, "dog": dog,
            "snk": snk, "cid": np.full((1, 4), float(c), np.float32),
        })

    res = run_bass_kernel_spmd(nc, in_maps, core_ids=list(range(NCORES)))
    y = np.concatenate([r["y"] for r in res.results], axis=0)
    return y.reshape(1, 8192, DM)
